# version 04 (final)
# baseline (speedup 1.0000x reference)
import contextlib
import numpy as np
import concourse.bass as bass
import concourse.mybir as mybir
from concourse.bass_utils import run_bass_kernel_spmd

F32 = mybir.dt.float32
BF16 = mybir.dt.bfloat16
U32 = mybir.dt.uint32
AF = mybir.ActivationFunctionType
ALU = mybir.AluOpType
AX = mybir.AxisListType

T, D, H, E, FF = 4096, 1024, 16, 8, 4096
NT, NC, NF = T // 128, D // 128, FF // 128
NG = NT // 4
CLIST = [0, 1, 2, 4, 8, 16]
KWIN = 20
LN_EPS = 1e-5
INV_D = 1.0 / D
NRUN = NT * E + E
NRECS = NT * E * 128 + 2048
NPTILE = 2 * T // 128 + 2 * E
NPAIR = NPTILE // 2
GSCALE = float(1 << 24)
OOBD = 1 << 20


def _block_masks():
    offs = {1 << k for k in range(12)}
    ms = []
    for c in CLIST:
        m = np.zeros((128, 128), dtype=np.float32)
        for a in range(128):
            for b in range(128):
                d = a - b + 128 * c
                if d in offs or (d == 0 and c == 0):
                    m[a, b] = 1.0
        ms.append(m.T.copy())
    return np.stack(ms)


def build_program(debug_taps=(), stop_after=None, p0_cut=99):
    nc = bass.Bass("TRN2", target_bir_lowering=False, debug=False)

    dx = nc.dram_tensor("x", [T, D], F32, kind="ExternalInput").ap()
    dW = {w: nc.dram_tensor("W" + w, [D, D], F32, kind="ExternalInput").ap()
          for w in ("q", "k", "v", "o")}
    db = {w: nc.dram_tensor("b" + w, [D], F32, kind="ExternalInput").ap()
          for w in ("q", "k", "v", "o")}
    dgam = nc.dram_tensor("gamma", [D], F32, kind="ExternalInput").ap()
    dbet = nc.dram_tensor("beta", [D], F32, kind="ExternalInput").ap()
    dWr = nc.dram_tensor("Wr", [D, E], F32, kind="ExternalInput").ap()
    dbr = nc.dram_tensor("br", [E], F32, kind="ExternalInput").ap()
    dW1 = nc.dram_tensor("W1", [E, D, FF], F32, kind="ExternalInput").ap()
    db1 = nc.dram_tensor("b1", [E, FF], F32, kind="ExternalInput").ap()
    dW2 = nc.dram_tensor("W2", [E, FF, D], F32, kind="ExternalInput").ap()
    db2 = nc.dram_tensor("b2", [E, D], F32, kind="ExternalInput").ap()
    dout = nc.dram_tensor("out", [T, D], F32, kind="ExternalOutput").ap()

    s_ln1 = nc.dram_tensor("s_ln1", [NT, 128, D], BF16, kind="Internal").ap()
    s_vext = nc.dram_tensor("s_vext", [NT, 128, H * 65], BF16, kind="Internal").ap()
    s_x2 = nc.dram_tensor("s_x2", [NT, 128, D], F32, kind="Internal").ap()
    s_ln2 = nc.dram_tensor("s_ln2", [NT, 128, D], F32, kind="Internal").ap()
    s_xn = nc.dram_tensor("s_xn", [T, D], BF16, kind="Internal").ap()
    s_wqkT = nc.dram_tensor("s_wqkT", [2, NC, 128, D], BF16, kind="Internal").ap()
    s_wvoT = nc.dram_tensor("s_wvoT", [2, NC, 128, D], BF16, kind="Internal").ap()
    s_w1s = nc.dram_tensor("s_w1s", [E * 128, NC * FF], BF16, kind="Internal").ap()
    s_w2s = nc.dram_tensor("s_w2s", [E * 128, NF * D], BF16, kind="Internal").ap()
    s_b1T = nc.dram_tensor("s_b1T", [E * 128, NF], F32, kind="Internal").ap()
    s_cc = nc.dram_tensor("s_cc", [E + 1, D], F32, kind="Internal").ap()
    s_recs = nc.dram_tensor("s_recs", [NRECS, 4], U32, kind="Internal").ap()
    s_runtab = nc.dram_tensor("s_runtab", [NRUN, 2], F32, kind="Internal").ap()
    s_adj = nc.dram_tensor("s_adj", [NT, H * 65], F32, kind="Internal").ap()
    s_tcs = nc.dram_tensor("s_tcs", [NT, H * 65], F32, kind="Internal").ap()
    s_pairs = nc.dram_tensor("s_pairs", [2 * T, D], F32, kind="Internal").ap()

    c_masks = nc.inline_tensor(_block_masks(), name="cmasks").ap()
    c_ident = nc.inline_tensor(np.eye(128, dtype=np.float32), name="cident").ap()
    c_nego = nc.inline_tensor(-np.ones((128, 128), np.float32), name="cnego").ap()
    c_ones1 = nc.inline_tensor(np.ones((1, 128), np.float32), name="cones1").ap()
    c_LT = nc.inline_tensor(np.triu(np.ones((128, 128), np.float32), 1), name="cLT").ap()
    c_U8 = nc.inline_tensor(np.triu(np.ones((8, 8), np.float32), 1), name="cU8").ap()
    c_iota = nc.inline_tensor(np.arange(128, dtype=np.float32).reshape(128, 1),
                              name="ciota").ap()
    c_evec = nc.inline_tensor(
        np.tile(np.arange(8, dtype=np.float32), (128, 1)), name="cevec").ap()
    rs = np.zeros((1, NRUN), np.float32)
    for e in range(E):
        for i in range(NT):
            rs[0, e * (NT + 1) + i] = (i * E + e) * 128
        rs[0, e * (NT + 1) + NT] = NT * E * 128
    c_runstart = nc.inline_tensor(rs, name="crunstart").ap()
    _sm = np.ones((NT, NT), np.float32)
    for _i in range(NT):
        for _c in CLIST:
            if _i >= _c:
                _sm[_i - _c, _i] = 0.0
    c_S32 = nc.inline_tensor(_sm, name="cS32").ap()

    dbg = {}

    def tap(name, shape, dtype=F32):
        if name in debug_taps:
            dbg[name] = nc.dram_tensor("dbg_" + name, shape, dtype,
                                       kind="ExternalOutput").ap()
        return dbg.get(name)

    t_ln1 = tap("ln1", [NT, 128, D], BF16)
    t_vext = tap("vext", [NT, 128, H * 65], BF16)
    t_vsum = tap("vsum", [1, H * 65])
    t_ao = tap("ao", [NT, 128, D], BF16)
    t_x2 = tap("x2", [NT, 128, D])
    t_ln2 = tap("ln2", [NT, 128, D])
    t_g = tap("g", [128, NT * E])
    t_G = tap("G", [128, NT])
    t_cnt = tap("cnt", [1, NT * E])
    t_runtab = tap("runtab", [NRUN, 2])
    t_cc = tap("cc", [E + 1, D])
    t_pairs = tap("pairs", [2 * T, D])

    ctx = contextlib.ExitStack()
    with ctx:
        EE = ctx.enter_context

        class Sem:
            def __init__(self, name):
                self.h = EE(nc.semaphore(name))
                self.val = 0

        STREAM_NAMES = [
            "cst", "row", "wstg", "wst", "wv8", "x0", "x1", "stl0", "stl1",
            "stv0", "stv1", "tap", "lg0", "lg1", "vr", "ws0", "ws1", "wo",
            "wc", "sx0", "sx1", "sl20", "sl21", "sxn0", "sxn1", "ta0", "ta1",
            "sc", "cst2", "rt", "rp", "rc", "xp0", "xp1", "xp2", "xp3", "wg",
            "st0", "st1", "ce", "pl0", "pl1", "ccl", "ou0", "ou1", "cstg",
            "taps", "aj0", "aj1", "tc0", "tc1", "vr0", "vr1"]
        STREAMS = {n: Sem("q_" + n) for n in STREAM_NAMES}
        MM, VE, ACS, POG, SYN, BAR = (Sem("mm"), Sem("ve"), Sem("ac"), Sem("po"),
                                      Sem("sy"), Sem("bar"))
        CHPOOL = [{e: EE(nc.semaphore(f"ch{p}_{e}"))
                   for e in ("sync", "gpsimd", "tensor", "vector", "scalar")}
                  for p in range(8)]
        ALLSEMS = list(STREAMS.values()) + [MM, VE, ACS, POG, SYN]
        MSEM = {"tensor": MM, "vector": VE, "scalar": ACS, "gpsimd": POG,
                "sync": SYN}

        def sbuf(name, shape, dtype, st=None):
            return (st or ctx).enter_context(nc.sbuf_tensor(name, shape, dtype))[:]

        b_identf = sbuf("b_identf", [128, 128], F32)
        b_eps = sbuf("b_eps", [128, 1], F32)
        b_identb = sbuf("b_identb", [128, 128], BF16)
        b_ones1b = sbuf("b_ones1b", [1, 128], BF16)
        b_ones1f = sbuf("b_ones1f", [1, 128], F32)
        b_onescolb = sbuf("b_onescolb", [128, 1], BF16)
        b_onescolf = sbuf("b_onescolf", [128, 1], F32)
        b_iota = sbuf("b_iota", [128, 1], F32)
        b_wr = sbuf("b_wr", [128, NC * E], BF16)
        b_brrep = sbuf("b_brrep", [128, E], F32)
        b_g = sbuf("b_g", [128, NT * E], F32)
        b_G = sbuf("b_G", [128, NT], F32)
        b_cnt = sbuf("b_cnt", [1, NT * E], F32)
        b_stat = sbuf("b_stat", [128, 16], F32)
        b_r8 = sbuf("b_r8", [128, E], F32)
        b_r8b = sbuf("b_r8b", [128, E], F32)
        b_r8c = sbuf("b_r8c", [128, E], F32)

        block = EE(nc.Block())
        ENGINES = ["sync", "gpsimd", "tensor", "vector", "scalar"]
        phase_no = [0]

        def run_phase(builder):
            with contextlib.ExitStack() as ph:
                body = builder(ph)
                snap = [s.val for s in ALLSEMS]

                def walk(en, eg):
                    for s, v in zip(ALLSEMS, snap):
                        s.val = v
                    eg.wait_ge(BAR.h, 5 * phase_no[0])
                    issued = {}

                    class He:
                        pass

                    he = He()
                    he.en = en
                    he.eg = eg
                    wm = {}

                    def own(name):
                        return en == name

                    def do_waits(waits):
                        for wv in waits:
                            if not wv:
                                continue
                            s, v = wv
                            if v > 0 and v > wm.get(id(s), -1):
                                eg.wait_ge(s.h, v)
                                wm[id(s)] = v

                    def dma(queue, stream, out, in_, waits=(), **kw):
                        sem = STREAMS[stream]
                        if en == queue:
                            do_waits(waits)
                            eg.dma_start(out, in_, **kw).then_inc(sem.h, 16)
                        sem.val += 16
                        issued[(queue, stream)] = sem.val
                        return (sem, sem.val)

                    bc_regs = {}

                    def idma(stream, out, out_off, in_, in_off, waits=(), **kw):
                        sem = STREAMS[stream]
                        if en == "gpsimd":
                            do_waits(waits)
                            bc = kw.get("bounds_check")
                            if isinstance(bc, int):
                                if bc not in bc_regs:
                                    bc_regs[bc] = eg.to_reg(bc)
                                kw["bounds_check"] = bc_regs[bc]
                            eg.indirect_dma_start(
                                out, out_off, in_, in_off, **kw).then_inc(sem.h, 16)
                        sem.val += 16
                        issued[("gpsimd", stream)] = sem.val
                        return (sem, sem.val)

                    chsem = CHPOOL[phase_no[0]][en]
                    chv = [0]

                    def C(inst):
                        chv[0] += 1
                        inst.then_inc(chsem, 1)
                        eg.wait_ge(chsem, chv[0])
                        return inst
                    he.C = C

                    def step(engine, waits, fn):
                        sem = MSEM[engine]
                        if en == engine:
                            do_waits(waits)
                            if sem.val > 0:
                                eg.wait_ge(sem.h, sem.val)
                            r = fn(eg)
                            insts = list(r) if isinstance(r, (list, tuple)) else [r]
                            insts = [i for i in insts if i is not None]
                            if not insts:
                                insts = [eg.nop()]
                            assert len(insts) == 1, "chain multi-op fns with he.C"
                            insts[-1].then_inc(sem.h, 1)
                        sem.val += 1
                        return (sem, sem.val)
                    he.do_waits = do_waits

                    he.own, he.dma, he.idma, he.step = own, dma, idma, step
                    body(he)
                    for (q, stream), v in issued.items():
                        if en == q:
                            eg.wait_ge(STREAMS[stream].h, v)
                    eg.nop().then_inc(BAR.h, 1)

                for en in ENGINES:
                    getattr(block, en)(lambda eg, en=en: walk(en, eg))
                BAR.val += 5
                phase_no[0] += 1

        def phase0(ph):
            b_wstg = sbuf("p0_wstg", [128, D], BF16, ph)
            b_wstg2 = sbuf("p0_wstg2", [128, D], BF16, ph)
            b_jstg = [sbuf(f"p0_jstg{j}", [128, D], BF16, ph) for j in range(NC)]
            b_row1 = sbuf("p0_row1", [1, D], F32, ph)
            b_b1s = sbuf("p0_b1s", [128, 128], F32, ph)
            b_b1t = sbuf("p0_b1t", [128, 32], F32, ph)
            p_rep = ph.enter_context(nc.psum_tensor("p0_rep", [128, 512], F32))[:]
            p_tp2 = [ph.enter_context(
                nc.psum_tensor(f"p0_tp{i}", [128, 128], BF16))[:] for i in range(2)]
            p_tpf = ph.enter_context(nc.psum_tensor("p0_tpf", [128, 128], F32))[:]

            def body(he):
                own, dma, idma, step = he.own, he.dma, he.idma, he.step
                C = he.C
                dma("sync", "cst", b_masks.rearrange("p (c l) -> p c l", l=128),
                    c_masks.rearrange("c m l -> m c l"))
                dma("gpsimd", "cstg", b_identb, c_ident)
                dma("gpsimd", "cstg", b_ones1b, c_ones1)
                dma("sync", "cst", b_ones1f, c_ones1)
                dma("sync", "cst", b_onescolf, c_ones1.rearrange("o p -> p o"),
                    allow_slow_non_contiguous=True)
                dma("sync", "cst", b_iota, c_iota)
                dma("sync", "cst", b_evec, c_evec)
                dma("sync", "cst", b_LT, c_LT)
                dma("sync", "cst", b_U8, c_U8)
                dma("sync", "cst", b_identf, c_ident)
                dma("sync", "cst", b_bq, db["q"].rearrange("(c p) -> p c", p=128),
                    allow_slow_non_contiguous=True)
                dma("sync", "cst", b_bk, db["k"].rearrange("(c p) -> p c", p=128),
                    allow_slow_non_contiguous=True)
                cstg_all = dma("gpsimd", "cstg",
                               b_wr.rearrange("p (c e) -> p c e", e=E),
                               dWr.rearrange("(c p) e -> p c e", p=128),
                               allow_slow_non_contiguous=True)
                cst_all = (STREAMS["cst"], STREAMS["cst"].val)
                step("gpsimd", [], lambda eg: (
                    C(eg.memset(b_onescolb, 1.0)),
                    eg.memset(b_eps, LN_EPS))[-1])
                if p0_cut < 2:
                    return

                for rn, src, dst in [
                        ("bv", db["v"], ROWS["bv"]), ("bo", db["o"], ROWS["bo"]),
                        ("gam", dgam, ROWS["gam"]), ("bet", dbet, ROWS["bet"])]:
                    v = dma("sync", "row", b_row1, src.rearrange("(o d) -> o d", o=1),
                            waits=[(MM, MM.val)])
                    for hf in range(2):
                        mv = step("tensor", [v, cst_all, cstg_all],
                                  lambda eg, hf=hf: eg.matmul(
                            p_rep, b_ones1f, b_row1[0:1, hf * 512:(hf + 1) * 512],
                            start=True, stop=True))
                        ev = step("vector", [mv], lambda eg, hf=hf, dst=dst:
                                  eg.tensor_copy(dst[:, hf * 512:(hf + 1) * 512], p_rep))
                        step("tensor", [ev], lambda eg: None)
                v = dma("sync", "row", b_row1[0:1, 0:E],
                        dbr.rearrange("(o e) -> o e", o=1), waits=[(MM, MM.val)])
                mv = step("tensor", [v], lambda eg: eg.matmul(
                    p_rep[:, 0:E], b_ones1f, b_row1[0:1, 0:E], start=True, stop=True))
                ev = step("vector", [mv], lambda eg:
                          eg.tensor_copy(b_brrep, p_rep[:, 0:E]))
                step("tensor", [ev], lambda eg: None)

                if p0_cut < 3:
                    return
                nweights = min(4, p0_cut - 30) if 30 <= p0_cut < 40 else \
                    (1 if 20 <= p0_cut < 30 else 4)
                jstg_st = [None]
                evs = []
                prev_st = None
                for wname, dst, major in [("q", s_wqkT[0], "oc"), ("k", s_wqkT[1], "oc"),
                                          ("v", s_wvoT[0], "jc"), ("o", s_wvoT[1], "jc")][:nweights]:
                    for oc in range(NC):
                        v = dma("gpsimd", "wstg",
                                b_wstg.rearrange("p (s f) -> p s f", f=512),
                                dW[wname][oc * 128:(oc + 1) * 128, :]
                                .rearrange("p (s f) -> p s f", f=512),
                                waits=[(MM, MM.val)])
                        if p0_cut == 21:
                            continue
                        for jc in range(NC):
                            sl = len(evs) % 2
                            w = [v]
                            if len(evs) >= 2:
                                w.append(evs[-2])
                            mv = step("tensor", w, lambda eg, sl=sl, jc=jc: eg.transpose(
                                p_tp2[sl], b_wstg[:, jc * 128:(jc + 1) * 128],
                                b_identb))
                            if p0_cut == 22:
                                evs.append(mv)
                                continue
                            if major == "oc":
                                dcb, dcc = b_wstg2, jc
                            else:
                                dcb, dcc = b_jstg[jc], oc
                            w2 = [mv]
                            if major == "oc" and jc == 0:
                                w2.append(prev_st)
                            if major == "jc" and oc == 0 and jc == 0:
                                w2.append(jstg_st[0])
                            ev = step("vector", w2, lambda eg, dcb=dcb, dcc=dcc, sl=sl:
                                      eg.tensor_copy(dcb[:, dcc * 128:(dcc + 1) * 128],
                                                     p_tp2[sl]))
                            evs.append(ev)
                        if major == "oc" and p0_cut != 23 and p0_cut != 22 \
                                and p0_cut != 21:
                            prev_st = dma("scalar", "wst", dst[oc], b_wstg2,
                                          waits=[evs[-1]])
                    if major == "jc":
                        for jc in range(NC):
                            jstg_st[0] = dma("scalar", "wst", dst[jc], b_jstg[jc],
                                             waits=[evs[-1]])

                if p0_cut < 4 or (20 <= p0_cut < 40):
                    return
                step("vector", [], lambda eg: eg.memset(b_b1s, 0.0))
                prev_st = None
                for e in range(E):
                    v = dma("sync", "row", b_b1s[0:32, :],
                            db1[e].rearrange("(c p) -> c p", p=128),
                            waits=[(VE, VE.val), prev_st, (MM, MM.val)])
                    mv = step("tensor", [v, cst_all], lambda eg: eg.transpose(
                        p_tpf, b_b1s, b_identf))
                    ev = step("vector", [mv], lambda eg:
                              eg.tensor_copy(b_b1t, p_tpf[:, 0:32]))
                    prev_st = dma("scalar", "wst", s_b1T[e * 128:(e + 1) * 128, :],
                                  b_b1t, waits=[ev])

            return body

        mid = ctx.enter_context(contextlib.ExitStack())
        b_masks = sbuf("b_masks", [128, 6 * 128], F32, mid)
        b_evec = sbuf("b_evec", [128, 8], F32, mid)
        b_LT = sbuf("b_LT", [128, 128], F32, mid)
        b_U8 = sbuf("b_U8", [8, 8], F32, mid)
        b_bq = sbuf("b_bq", [128, NC], F32, mid)
        b_bk = sbuf("b_bk", [128, NC], F32, mid)
        ROWS = {r: sbuf("b_row_" + r, [128, D], F32, mid)
                for r in ("bv", "bo", "gam", "bet")}

        def phaseNul(ph):
            def body(he):
                pass
            return body

        if stop_after == "nul":
            run_phase(phaseNul)
            return nc, dbg
        run_phase(phase0)

        def ln_seq(he, src, dst, sq_junk, waits=(), sums_precomputed=False):
            step, C = he.step, he.C

            def stats(eg):
                if not sums_precomputed:
                    C(eg.reduce_sum(b_stat[:, 0:1], src, axis=AX.X))
                    C(eg.scalar_tensor_tensor(
                        sq_junk, src, 1.0, src, ALU.mult, ALU.mult,
                        accum_out=b_stat[:, 1:2]))
                C(eg.tensor_scalar(b_stat[:, 2:3], b_stat[:, 0:1], INV_D,
                                   None, ALU.mult))
                C(eg.tensor_scalar(b_stat[:, 3:4], b_stat[:, 1:2], INV_D,
                                   None, ALU.mult))
                C(eg.tensor_tensor(b_stat[:, 4:5], b_stat[:, 2:3],
                                   b_stat[:, 2:3], ALU.mult))
                return eg.tensor_tensor(b_stat[:, 5:6], b_stat[:, 3:4],
                                        b_stat[:, 4:5], ALU.subtract)
            v1 = step("vector", list(waits), stats)
            a1 = step("scalar", [v1], lambda eg: eg.activation(
                b_stat[:, 6:7], b_stat[:, 5:6], AF.Sqrt, bias=b_eps))

            def norm(eg):
                C(eg.reciprocal(b_stat[:, 7:8], b_stat[:, 6:7]))
                return eg.tensor_scalar(dst, src, b_stat[:, 2:3], b_stat[:, 7:8],
                                        ALU.subtract, ALU.mult)
            return step("vector", [a1], norm)

        def phaseV(ph):
            b_x = [sbuf(f"pv_x{i}", [128, D], F32, ph) for i in range(2)]
            b_ln1 = [sbuf(f"pv_ln1{i}", [128, D], BF16, ph) for i in range(2)]
            b_sq = sbuf("pv_sq", [128, D], F32, ph)
            b_ln1T = [sbuf(f"pv_ln1T{c}", [128, 512], BF16, ph) for c in range(NC)]
            b_wv8 = sbuf("pv_wv8", [128, NC * D], BF16, ph)
            b_vstg = [sbuf(f"pv_vstg{i}", [128, H * 65], BF16, ph) for i in range(2)]
            b_vsacc = sbuf("pv_vsacc", [1, 2048], F32, ph)
            b_tcs = sbuf("pv_tcs", [NT, H * 65], F32, ph)
            b_S32 = sbuf("pv_S32", [NT, NT], F32, ph)
            b_vstmp = [sbuf(f"pv_vt{i}", [1, H * 65], F32, ph) for i in range(2)]
            b_vsrow = (sbuf("pv_vsrow", [1, H * 65], F32, ph)
                       if "vsum" in debug_taps else None)
            p_proj = [ph.enter_context(
                nc.psum_tensor(f"pv_pj{i}", [128, 512], F32))[:] for i in range(2)]
            p_tp2 = [ph.enter_context(
                nc.psum_tensor(f"pv_tp{i}", [128, 128], BF16))[:] for i in range(2)]
            p_vs = ph.enter_context(nc.psum_tensor("pv_vs", [1, 2048], F32))[:]

            def body(he):
                own, dma, idma, step = he.own, he.dma, he.idma, he.step
                C = he.C
                wv_ld = None
                for jc in range(NC):
                    wv_ld = dma("sync", "wv8", b_wv8[:, jc * D:(jc + 1) * D],
                                s_wvoT[0, jc])
                s32_ld = dma("sync", "cst", b_S32, c_S32)
                ln1_done, st_ln1, tp_last = {}, {}, {}
                tpev = {}
                tplist = []
                pj_ev = []
                vst_free = {}
                vsadd = {}
                tcs_st = {}
                vproj_last = [None]
                for g in range(NG):
                    for tb in range(4):
                        i = g * 4 + tb
                        bx, bl = b_x[i % 2], b_ln1[i % 2]
                        xv = dma("sync", "x%d" % (i % 2), bx,
                                 dx[i * 128:(i + 1) * 128, :],
                                 waits=[ln1_done.get(i - 2)])
                        lnw = [xv, st_ln1.get(i - 2), tp_last.get(i - 2)]
                        ln1_done[i] = ln_seq(he, bx, bl, b_sq, waits=lnw)
                        st_ln1[i] = dma("scalar", "stl%d" % (i % 2), s_ln1[i], bl,
                                        waits=[ln1_done[i]])
                        if t_ln1 is not None:
                            st_ln1[i] = dma("scalar", "stl%d" % (i % 2),
                                            t_ln1[i], bl)
                        for jc in range(NC):
                            sl = len(tplist) % 2
                            w = [ln1_done[i],
                                 tplist[-2] if len(tplist) >= 2 else None]
                            mv = step("tensor", w, lambda eg, jc=jc, bl=bl, sl=sl:
                                      eg.transpose(
                                          p_tp2[sl],
                                          bl[:, jc * 128:(jc + 1) * 128], b_identb))
                            tpev[(i, jc)] = step(
                                "vector", [mv, vproj_last[0]],
                                lambda eg, jc=jc, tb=tb, sl=sl: eg.tensor_copy(
                                    b_ln1T[jc][:, tb * 128:(tb + 1) * 128],
                                    p_tp2[sl]))
                            tplist.append(tpev[(i, jc)])
                            tp_last[i] = mv
                    for tb in range(4):
                        i = g * 4 + tb
                        bst = b_vstg[i % 2]
                        for hf in range(2):
                            k = (tb * 2 + hf) % 2
                            w = [wv_ld, vst_free.get(i - 2)]
                            w += [tpev[(i, jc)] for jc in range(NC)]
                            if len(pj_ev) >= 2:
                                w.append(pj_ev[-2])

                            def vproj(eg, tb=tb, hf=hf, k=k):
                                r = None
                                for jc in range(NC):
                                    r = eg.matmul(
                                        p_proj[k],
                                        b_ln1T[jc][:, tb * 128:(tb + 1) * 128],
                                        b_wv8[:, jc * D + hf * 512:
                                              jc * D + (hf + 1) * 512],
                                        start=(jc == 0), stop=(jc == NC - 1))
                                return r
                            mv = step("tensor", w, vproj)
                            vproj_last[0] = mv

                            def vevac(eg, bst=bst, hf=hf, k=k):
                                r = None
                                for hh in range(8):
                                    h = hf * 8 + hh
                                    if r is not None:
                                        C(r)
                                    r = eg.scalar_tensor_tensor(
                                        bst[:, h * 65: h * 65 + 64],
                                        p_proj[k][:, hh * 64:(hh + 1) * 64], 1.0,
                                        ROWS["bv"][:, h * 64:(h + 1) * 64],
                                        ALU.mult, ALU.add)
                                return r
                            ev = step("vector", [mv], vevac)
                            pj_ev.append(ev)
                        onesv = step("vector", [], lambda eg, bst=bst: eg.memset(
                            bst.rearrange("p (h k) -> p h k", k=65)[:, :, 64:65], 1.0))
                        stv = dma("scalar", "stv%d" % (i % 2), s_vext[i], bst,
                                  waits=[onesv])
                        if t_vext is not None:
                            stv = dma("scalar", "stv%d" % (i % 2), t_vext[i], bst)

                        def colsum(eg, bst=bst, i=i):
                            r = None
                            for h in range(H):
                                if r is not None:
                                    pass
                                r = eg.matmul(
                                    p_vs[0:1, h * 128: h * 128 + 65],
                                    b_onescolb, bst[:, h * 65:(h + 1) * 65],
                                    start=True, stop=True)
                            return r
                        mc = step("tensor", [onesv, vsadd.get(i - 1)], colsum)

                        def accum(eg, i=i):
                            va = b_vsacc.rearrange("o (h k) -> o h k", k=128)[:, :, 0:65]
                            pv = p_vs.rearrange("o (h k) -> o h k", k=128)[:, :, 0:65]
                            C(eg.tensor_copy(
                                b_vstmp[i % 2].rearrange("o (h k) -> o h k", k=65),
                                pv))
                            if i == 0:
                                C(eg.tensor_copy(va, pv))
                            else:
                                C(eg.tensor_tensor(va, va, pv, ALU.add))
                            return eg.nop()
                        vsadd[i] = step("vector",
                                        [mc, tcs_st.get(i - 2)], accum)
                        tcs_st[i] = dma("scalar", "tc%d" % (i % 2),
                                        s_tcs[i:i + 1], b_vstmp[i % 2],
                                        waits=[vsadd[i]])
                        vst_free[i] = step("vector", [stv, mc], lambda eg: None)
                if t_vsum is not None:
                    vs = step("vector", [(MM, MM.val)], lambda eg: eg.tensor_copy(
                        b_vsrow.rearrange("o (h k) -> o h k", k=65),
                        b_vsacc.rearrange("o (h k) -> o h k", k=128)[:, :, 0:65]))
                    dma("scalar", "tap", t_vsum, b_vsrow, waits=[vs])
                tcld = dma("sync", "cst", b_tcs, s_tcs,
                           waits=[tcs_st[NT - 1], tcs_st[NT - 2], s32_ld])
                prev_ev = None
                for ch in range(3):
                    c0 = ch * 512
                    cw = min(512, H * 65 - c0)
                    mm = step("tensor",
                              [s32_ld, tcld, prev_ev,
                               pj_ev[-1], pj_ev[-2]],
                              lambda eg, c0=c0, cw=cw: eg.matmul(
                                  p_proj[0][0:NT, 0:cw], b_S32,
                                  b_tcs[0:NT, c0:c0 + cw],
                                  start=True, stop=True))
                    prev_ev = step("vector", [mm],
                                   lambda eg, c0=c0, cw=cw: eg.tensor_copy(
                                       b_tcs[0:NT, c0:c0 + cw],
                                       p_proj[0][0:NT, 0:cw]))
                dma("sync", "cst", s_adj, b_tcs, waits=[prev_ev, tcld])
            return body

        if stop_after == '0':
            return nc, dbg
        run_phase(phaseV)

        wconv_units = []
        for e in range(E):
            for jc in range(NC):
                wconv_units.append((
                    s_w1s[e * 128:(e + 1) * 128, jc * FF:(jc + 1) * FF]
                    .rearrange("p (s f) -> p s f", f=2048),
                    dW1[e, jc * 128:(jc + 1) * 128, :]
                    .rearrange("p (s f) -> p s f", f=2048)))
        for e in range(E):
            for fq in range(4):
                wconv_units.append((
                    s_w2s[e * 128:(e + 1) * 128, fq * 8192:(fq + 1) * 8192]
                    .rearrange("p (c o) -> p c o", o=D),
                    dW2[e, fq * 1024:(fq + 1) * 1024, :]
                    .rearrange("(c p) o -> p c o", p=128)))

        def phaseA(ph):
            b_ln1g = [sbuf(f"pa_l1{i}", [128, D], BF16, ph) for i in range(2)]
            b_LX = [sbuf(f"pa_lx{c}", [128, 512], BF16, ph) for c in range(NC)]
            b_qT = [sbuf(f"pa_qT{c}", [128, 512], BF16, ph) for c in range(NC)]
            b_kT = [sbuf(f"pa_kT{c}", [128, KWIN * 128], BF16, ph) for c in range(NC)]
            b_vring = [sbuf(f"pa_vr{m}", [128, H * 65], BF16, ph) for m in range(KWIN)]
            b_wsl = [sbuf(f"pa_wsl{i}", [128, D], BF16, ph) for i in range(2)]
            b_wo8 = sbuf("pa_wo8", [128, NC * D], BF16, ph)
            b_wpre = [sbuf(f"pa_wp{i}", [128, 768], BF16, ph) for i in range(2)]
            b_aexp = [sbuf(f"pa_ae{i}", [128, 768], BF16, ph) for i in range(2)]
            b_ao = [sbuf(f"pa_ao{i}", [128, D], BF16, ph) for i in range(2)]
            b_T8 = [sbuf(f"pa_t8{c}", [128, 512], BF16, ph) for c in range(NC)]
            b_x = [sbuf(f"pa_x{i}", [128, D], F32, ph) for i in range(2)]
            b_o = sbuf("pa_o", [128, D], F32, ph)
            b_t0 = [sbuf(f"pa_t0{i}", [128, D], F32, ph) for i in range(2)]
            b_t1 = [sbuf(f"pa_t1{i}", [128, D], F32, ph) for i in range(2)]
            b_xn = sbuf("pa_xn", [128, D], F32, ph)
            b_xnb = [sbuf(f"pa_xnb{i}", [128, D], BF16, ph) for i in range(2)]
            b_gf = sbuf("pa_gf", [128, 128], F32, ph)
            b_fT = sbuf("pa_fT", [128, 128], F32, ph)
            b_rc = sbuf("pa_rc", [128, 8], F32, ph)
            b_rcu0 = sbuf("pa_rcu0", [128, 4], U32, ph)
            b_rcu1 = sbuf("pa_rcu1", [128, 4], U32, ph)
            b_sidx0 = sbuf("pa_sidx0", [128, 1], U32, ph)
            b_sidx1 = sbuf("pa_sidx1", [128, 1], U32, ph)
            p_QK = ph.enter_context(nc.psum_tensor("pa_QK", [128, 2048], F32))[:]
            p_AV = [ph.enter_context(
                nc.psum_tensor(f"pa_AV{i}", [128, 512], F32))[:] for i in range(2)]
            p_tp2 = [ph.enter_context(
                nc.psum_tensor(f"pa_tp{i}", [128, 128], BF16))[:] for i in range(2)]
            p_R = p_QK
            p_tpf = p_QK[:, 640:768]
            p_cum = p_QK[:, 776:784]
            b_adjq = [sbuf(f"pa_adj{i}", [1, 512], F32, ph) for i in range(2)]

            def body(he):
                own, dma, idma, step = he.own, he.dma, he.idma, he.step
                C = he.C
                eg = he.eg

                wo_ld = None
                for jc in range(NC):
                    wo_ld = dma("sync", "wo", b_wo8[:, jc * D:(jc + 1) * D],
                                s_wvoT[1, jc])
                ac_exp, ve_mask, av_m, qk_m = {}, {}, {}, {}
                ve_div, rk_ev, adq = {}, {}, {}
                vload_t = {}
                adjini = step("gpsimd", [], lambda eg: (
                    C(eg.memset(b_adjq[0], 0.0)),
                    eg.memset(b_adjq[1], 0.0))[-1])
                aotev, tp_ev = {}, {}
                ln1g_tp, ta_st, sl2st, sxn_st, xn_tp_last = {}, {}, {}, {}, {}
                tplist = []

                def transpose_pipe(src_slice, dst_slice, waits):
                    sl = len(tplist) % 2
                    w = list(waits) + [tplist[-2] if len(tplist) >= 2 else None]
                    mv = step("tensor", w, lambda eg: eg.transpose(
                        p_tp2[sl], src_slice, b_identb))
                    ev = step("vector", [mv], lambda eg: eg.tensor_copy(
                        dst_slice, p_tp2[sl]))
                    tplist.append(ev)
                    return mv, ev
                chain_end = {}
                gbar = None
                for g in range(NG):
                    if gbar is not None and not he.own("sync"):
                        he.do_waits(gbar)
                    for u in range(g * 12, min((g + 1) * 12, len(wconv_units))):
                        dma("gpsimd", "wc", wconv_units[u][0], wconv_units[u][1])
                    lnld, vload = {}, {}
                    for tb in range(4):
                        i = g * 4 + tb
                        lnld[tb] = dma("sync", "lg%d" % (tb % 2), b_ln1g[tb % 2],
                                       s_ln1[i],
                                       waits=[ln1g_tp.get(i - 2)])
                        for jc in range(NC):
                            mv, ev = transpose_pipe(
                                b_ln1g[tb % 2][:, jc * 128:(jc + 1) * 128],
                                b_LX[jc][:, tb * 128:(tb + 1) * 128],
                                [lnld[tb]])
                            tp_ev[(tb, jc)] = ev
                            ln1g_tp[i] = mv
                        vload[tb] = dma(
                            "sync", "vr%d" % (i % 2), b_vring[i % KWIN],
                            s_vext[i],
                            waits=[vload_t.get(i - 2),
                                   av_m.get((i - 4) * H + H - 1)])
                        vload_t[i] = vload[tb]
                    vload_all = [vload[2], vload[3]]
                    lxr = max([tp_ev[(tb, jc)] for tb in range(4)
                               for jc in range(NC)], key=lambda t: t[1])
                    pj_ev = []
                    pj_mv = []
                    for wsel, bias in ((0, b_bq), (1, b_bk)):
                        for oc in range(NC):
                            wv = dma("sync", "ws%d" % (oc % 2), b_wsl[oc % 2],
                                     s_wqkT[wsel, oc],
                                     waits=[pj_mv[-2] if len(pj_mv) >= 2 else None])
                            w = [wv, lxr]
                            if len(pj_ev) >= 2:
                                w.append(pj_ev[-2])

                            def proj(eg, oc=oc):
                                r = None
                                for jc in range(NC):
                                    r = eg.matmul(
                                        p_QK[:, (oc % 2) * 1024:
                                             (oc % 2) * 1024 + 512],
                                        b_wsl[oc % 2][:, jc * 128:(jc + 1) * 128],
                                        b_LX[jc], start=(jc == 0), stop=(jc == NC - 1))
                                return r
                            mv = step("tensor", w, proj)
                            pj_mv.append(mv)
                            if wsel == 0:
                                ev = step("vector", [mv],
                                          lambda eg, oc=oc: eg.tensor_scalar(
                                              b_qT[oc],
                                              p_QK[:, (oc % 2) * 1024:
                                                   (oc % 2) * 1024 + 512],
                                              b_bq[:, oc:oc + 1], None, ALU.add))
                            else:
                                base = (g * 4) % KWIN
                                ev = step("vector", [mv],
                                          lambda eg, oc=oc, base=base:
                                          eg.tensor_scalar(
                                              b_kT[oc][:, base * 128:
                                                       base * 128 + 512],
                                              p_QK[:, (oc % 2) * 1024:
                                                   (oc % 2) * 1024 + 512],
                                              b_bk[:, oc:oc + 1], None, ALU.add))
                            pj_ev.append(ev)
                    pjr = pj_ev[-1]
                    for tb in range(4):
                        i = g * 4 + tb
                        valid = [c for c in CLIST if i >= c]
                        nv = len(valid)
                        def adjdma(q, i=i):
                            pi, pq = (i, q - 2) if q >= 2 else (i - 1, q + 2)
                            adq[(i, q)] = dma(
                                "sync", "aj%d" % (q % 2),
                                b_adjq[q % 2].rearrange(
                                    "o (h k) -> o h k", k=128)[:, :, 0:65],
                                s_adj[i:i + 1].rearrange(
                                    "o (h k) -> o h k", k=65)[:, q * 4:(q + 1) * 4, :],
                                waits=[rk_ev.get((pi, pq)), adjini])
                        adjdma(0)
                        adjdma(1)

                        def rank1(q, i=i):
                            pi, pq = (i, q - 2) if q >= 2 else (i - 1, q + 2)
                            puhs = pi * H + pq * 4 + 3
                            w = [adq[(i, q)], ve_div.get(puhs)]
                            rk_ev[(i, q)] = step(
                                "tensor", w,
                                lambda eg, q=q, i=i: eg.matmul(
                                    p_AV[q % 2], b_ones1f, b_adjq[q % 2],
                                    start=True, stop=False))

                        def avgroup(h, i=i, valid=valid):
                            hs = i * H + h
                            slot = (h % 4) * 128
                            pav = p_AV[(h // 4) % 2]
                            w = vload_all + [ac_exp[hs], rk_ev[(i, h // 4)]]

                            def fn(eg, h=h, hs=hs, slot=slot, pav=pav,
                                   valid=valid, i=i):
                                r = None
                                for cp, c in enumerate(valid):
                                    r = eg.matmul(
                                        pav[:, slot:slot + 65],
                                        b_aexp[hs % 2][:, cp * 128:(cp + 1) * 128],
                                        b_vring[(i - c) % KWIN][:, h * 65:(h + 1) * 65],
                                        start=False,
                                        stop=(cp == len(valid) - 1
                                              and h % 4 == 3))
                                return r
                            av_m[hs] = step("tensor", w, fn)

                        def divides(q, i=i, tb=tb):
                            lasthq = i * H + q * 4 + 3
                            for hh in range(4):
                                h = q * 4 + hh
                                hs = i * H + h
                                slot = (h % 4) * 128
                                pav = p_AV[q % 2]
                                w = [av_m[lasthq],
                                     aotev.get((i - 2, NC - 1)),
                                     ta_st.get(i - 2)]

                                def div(eg, pav=pav, slot=slot, tb=tb, h=h):
                                    C(eg.reciprocal(b_stat[:, 8:9],
                                                    pav[:, slot + 64:slot + 65]))
                                    return eg.tensor_scalar(
                                        b_ao[tb % 2][:, h * 64:(h + 1) * 64],
                                        pav[:, slot:slot + 64],
                                        b_stat[:, 8:9], None, ALU.mult)
                                ve_div[hs] = step("vector", w, div)

                        for h in range(H):
                            hs = i * H + h
                            hc, hb = h // 2, h % 2
                            if h % 4 == 0:
                                rank1(h // 4)
                                if h // 4 < 2:
                                    adjdma(h // 4 + 2)
                            w = [ve_mask.get(hs - 2), pjr]

                            def qk(eg, i=i, tb=tb, hc=hc, hb=hb, valid=valid,
                                   h=h):
                                r = None
                                qb = (h % 2) * 1024
                                for cp, c in enumerate(valid):
                                    sl = (i - c) % KWIN
                                    r = eg.matmul(
                                        p_QK[:, qb + cp * 128:qb + (cp + 1) * 128],
                                        b_kT[hc][hb * 64:(hb + 1) * 64,
                                                 sl * 128:(sl + 1) * 128],
                                        b_qT[hc][hb * 64:(hb + 1) * 64,
                                                 tb * 128:(tb + 1) * 128],
                                        start=True, stop=True)
                                return r
                            qk_m[hs] = step("tensor", w, qk)
                            if h > 0:
                                avgroup(h - 1)
                            w = [qk_m[hs], ac_exp.get(hs - 2)]
                            ve_mask[hs] = step(
                                "vector", w,
                                lambda eg, hs=hs, nv=nv, h=h:
                                eg.scalar_tensor_tensor(
                                    b_wpre[hs % 2][:, 0:nv * 128],
                                    p_QK[:, (h % 2) * 1024:
                                         (h % 2) * 1024 + nv * 128], 0.125,
                                    b_masks[:, 0:nv * 128], ALU.mult, ALU.mult))
                            w = [ve_mask[hs], av_m.get(hs - 2)]
                            ac_exp[hs] = step(
                                "scalar", w,
                                lambda eg, hs=hs, nv=nv: eg.activation(
                                    b_aexp[hs % 2][:, 0:nv * 128],
                                    b_wpre[hs % 2][:, 0:nv * 128], AF.Exp))
                            if h % 4 == 0 and h > 0:
                                divides(h // 4 - 1)
                        avgroup(H - 1)
                        divides(3)
                        if t_ao is not None:
                            ta_st[i] = dma("scalar", "ta%d" % (tb % 2), t_ao[i],
                                           b_ao[tb % 2],
                                           waits=[ve_div[i * H + H - 1]])
                        for jc in range(NC):
                            mv, ev = transpose_pipe(
                                b_ao[tb % 2][:, jc * 128:(jc + 1) * 128],
                                b_T8[jc][:, tb * 128:(tb + 1) * 128],
                                [ve_div[i * H + H - 1]])
                            aotev[(i, jc)] = mv
                            tp_ev[("ao", i, jc)] = ev
                    last_div = ve_div[(g * 4 + 3) * H + H - 1]
                    oproj_ev = []
                    x2st = {}
                    for tb in range(4):
                        i = g * 4 + tb
                        for hf in range(2):
                            w = [last_div, wo_ld]
                            w += [tp_ev[("ao", i, jc)] for jc in range(NC)]
                            if len(oproj_ev) >= 2:
                                w.append(oproj_ev[-2])

                            kk = (tb * 2 + hf) % 2

                            def oproj(eg, tb=tb, hf=hf, kk=kk):
                                r = None
                                for jc in range(NC):
                                    r = eg.matmul(
                                        p_AV[kk][:, 0:512],
                                        b_T8[jc][:, tb * 128:(tb + 1) * 128],
                                        b_wo8[:, jc * D + hf * 512:
                                              jc * D + (hf + 1) * 512],
                                        start=(jc == 0), stop=(jc == NC - 1))
                                return r
                            mv = step("tensor", w, oproj)
                            ev = step("vector", [mv],
                                      lambda eg, tb=tb, hf=hf, kk=kk:
                                      eg.scalar_tensor_tensor(
                                          b_o[:, hf * 512:(hf + 1) * 512],
                                          p_AV[kk][:, 0:512], 1.0,
                                          ROWS["bo"][:, hf * 512:(hf + 1) * 512],
                                          ALU.mult, ALU.add,
                                          accum_out=b_stat[:, 8 + hf:9 + hf]))
                            oproj_ev.append(ev)
                        xv = dma("sync", "x%d" % (tb % 2), b_x[tb % 2],
                                 dx[i * 128:(i + 1) * 128, :],
                                 waits=[chain_end.get(i - 2)])
                        sq1 = step("vector", [], lambda eg: (
                            C(eg.tensor_tensor(b_stat[:, 0:1], b_stat[:, 8:9],
                                               b_stat[:, 9:10], ALU.add)),
                            eg.scalar_tensor_tensor(
                                b_xn, b_o, 1.0, b_o, ALU.mult, ALU.mult,
                                accum_out=b_stat[:, 1:2]))[-1])
                        y1 = ln_seq(he, b_o, b_xn, None, waits=[sq1],
                                    sums_precomputed=True)
                        x2v = step("vector", [xv, x2st.get(tb - 2)],
                                   lambda eg, tb=tb: eg.scalar_tensor_tensor(
                                       b_t0[tb % 2], b_xn, 1.0, b_x[tb % 2],
                                       ALU.mult, ALU.add,
                                       accum_out=b_stat[:, 0:1]))
                        x2st[tb] = dma("scalar", "sx%d" % (tb % 2), s_x2[i],
                                       b_t0[tb % 2], waits=[x2v])
                        if t_x2 is not None:
                            x2st[tb] = dma("scalar", "sx%d" % (tb % 2), t_x2[i],
                                           b_t0[tb % 2])
                        sq2 = step("vector", [], lambda eg, tb=tb:
                                   eg.scalar_tensor_tensor(
                                       b_xn, b_t0[tb % 2], 1.0, b_t0[tb % 2],
                                       ALU.mult, ALU.mult,
                                       accum_out=b_stat[:, 1:2]))
                        ln2v = ln_seq(he, b_t0[tb % 2], b_t1[tb % 2], None,
                                      waits=[sq2, sl2st.get(tb - 2)],
                                      sums_precomputed=True)
                        sl2st[tb] = dma("sync", "sl2%d" % (tb % 2), s_ln2[i],
                                        b_t1[tb % 2], waits=[ln2v])
                        if t_ln2 is not None:
                            sl2st[tb] = dma("sync", "sl2%d" % (tb % 2), t_ln2[i],
                                            b_t1[tb % 2])
                        xnv = step("vector", [sxn_st.get(tb - 2),
                                                xn_tp_last.get(tb - 2)],
                                   lambda eg, tb=tb: (
                            C(eg.tensor_tensor(b_xn, b_t1[tb % 2], ROWS["gam"],
                                               ALU.mult)),
                            eg.scalar_tensor_tensor(
                                b_xnb[tb % 2], b_xn, 1.0, ROWS["bet"],
                                ALU.mult, ALU.add))[-1])
                        sxn_st[tb] = dma("scalar", "sxn%d" % (tb % 2),
                                         s_xn[i * 128:(i + 1) * 128, :],
                                         b_xnb[tb % 2], waits=[xnv])
                        for jc in range(NC):
                            mv, ev = transpose_pipe(
                                b_xnb[tb % 2][:, jc * 128:(jc + 1) * 128],
                                b_LX[jc][:, tb * 128:(tb + 1) * 128],
                                [xnv])
                            xn_tp_last[tb] = mv
                            tp_ev[("xn", i, jc)] = ev
                        w = [chain_end.get(i - 1), last_div]
                        w += [tp_ev[("xn", i, jc)] for jc in range(NC)]

                        def router(eg, tb=tb):
                            r = None
                            for jc in range(NC):
                                r = eg.matmul(
                                    p_R[:, 0:8],
                                    b_LX[jc][:, tb * 128:(tb + 1) * 128],
                                    b_wr[:, jc * 8:(jc + 1) * 8],
                                    start=(jc == 0), stop=(jc == NC - 1))
                            return r
                        rt = step("tensor", w, router)
                        lg = step("vector", [rt], lambda eg: (
                            eg.tensor_tensor(b_r8, p_R[:, 0:8], b_brrep, ALU.add))
                        )
                        gv = step("vector", [], lambda eg, i=i: (
                            C(eg.reduce_max(b_stat[:, 10:11], b_r8, axis=AX.X)),
                            eg.tensor_scalar(b_stat[:, 11:12], b_stat[:, 10:11],
                                             -1.0, None, ALU.mult))[-1])
                        ex = step("scalar", [gv], lambda eg: eg.activation(
                            b_r8b, b_r8, AF.Exp, bias=b_stat[:, 11:12]))
                        def gates(eg, i=i):
                            C(eg.reduce_sum(b_stat[:, 10:11], b_r8b, axis=AX.X))
                            C(eg.reciprocal(b_stat[:, 11:12], b_stat[:, 10:11]))
                            C(eg.tensor_scalar(b_r8, b_r8b, b_stat[:, 11:12],
                                               None, ALU.mult))
                            C(eg.reduce_max(b_stat[:, 10:11], b_r8, axis=AX.X))
                            C(eg.tensor_scalar(b_r8b, b_r8, b_stat[:, 10:11],
                                               None, ALU.is_lt))
                            C(eg.tensor_tensor(b_r8b, b_r8, b_r8b, ALU.mult))
                            C(eg.reduce_max(b_stat[:, 12:13], b_r8b, axis=AX.X))
                            C(eg.tensor_scalar(b_r8c, b_r8, b_stat[:, 12:13],
                                               None, ALU.is_ge))
                            C(eg.tensor_tensor(b_g[:, i * 8:(i + 1) * 8], b_r8,
                                               b_r8c, ALU.mult))
                            return eg.reduce_sum(b_G[:, i:i + 1],
                                                 b_g[:, i * 8:(i + 1) * 8],
                                                 axis=AX.X)
                        gv2 = step("vector", [ex], gates)
                        pf = step("tensor", [gv2], lambda eg: eg.matmul(
                            p_R[:, 8:16], b_LT, b_r8c, start=True, stop=True))
                        cn = step("tensor", [], lambda eg: eg.matmul(
                            p_R[0:1, 512:520], b_onescolf, b_r8c,
                            start=True, stop=True))
                        cp = step("vector", [gv2], lambda eg: eg.tensor_copy(
                            b_gf[:, 0:8], b_r8c))
                        ft = step("tensor", [cp], lambda eg: eg.transpose(
                            p_tpf, b_gf, b_identf))
                        fe = step("vector", [ft], lambda eg: eg.tensor_copy(
                            b_fT[0:8, :], p_tpf[0:8, :]))
                        cu = step("tensor", [fe], lambda eg: eg.matmul(
                            p_cum, b_fT[0:8, :], b_U8,
                            start=True, stop=True))

                        def a4(eg, i=i):
                            P = p_R[:, 8:16]
                            A = C
                            A(eg.tensor_copy(b_cnt[0:1, i * 8:(i + 1) * 8],
                                             p_R[0:1, 512:520]))
                            cum = p_cum
                            A(eg.tensor_scalar(b_r8b, cum, 0.0, None, ALU.is_equal))
                            A(eg.tensor_tensor(b_r8b, b_r8b, b_r8c, ALU.mult))
                            A(eg.tensor_tensor(b_gf[:, 8:16], b_r8b, b_evec,
                                               ALU.mult))
                            A(eg.reduce_sum(b_stat[:, 10:11], b_gf[:, 8:16],
                                            axis=AX.X))
                            A(eg.tensor_tensor(b_gf[:, 8:16], b_r8b, P, ALU.mult))
                            A(eg.reduce_sum(b_stat[:, 11:12], b_gf[:, 8:16],
                                            axis=AX.X))
                            A(eg.tensor_tensor(b_gf[:, 8:16], b_r8b, b_r8,
                                               ALU.mult))
                            A(eg.reduce_sum(b_stat[:, 12:13], b_gf[:, 8:16],
                                            axis=AX.X))
                            A(eg.tensor_scalar(b_r8b, cum, 1.0, None, ALU.is_equal))
                            A(eg.tensor_tensor(b_r8b, b_r8b, b_r8c, ALU.mult))
                            A(eg.tensor_tensor(b_gf[:, 8:16], b_r8b, b_evec,
                                               ALU.mult))
                            A(eg.reduce_sum(b_stat[:, 13:14], b_gf[:, 8:16],
                                            axis=AX.X))
                            A(eg.tensor_tensor(b_gf[:, 8:16], b_r8b, P, ALU.mult))
                            A(eg.reduce_sum(b_stat[:, 14:15], b_gf[:, 8:16],
                                            axis=AX.X))
                            A(eg.tensor_tensor(b_gf[:, 8:16], b_r8b, b_r8,
                                               ALU.mult))
                            A(eg.reduce_sum(b_stat[:, 15:16], b_gf[:, 8:16],
                                            axis=AX.X))
                            A(eg.tensor_scalar(b_rc[:, 0:1], b_iota, float(i * 128),
                                               None, ALU.add))
                            A(eg.tensor_copy(b_rc[:, 1:2], b_rc[:, 0:1]))
                            A(eg.tensor_scalar(b_rc[:, 2:3], b_stat[:, 12:13],
                                               GSCALE, None, ALU.mult))
                            A(eg.tensor_copy(b_rc[:, 4:5], b_rc[:, 0:1]))
                            A(eg.tensor_scalar(b_rc[:, 5:6], b_rc[:, 0:1], float(T),
                                               None, ALU.add))
                            A(eg.tensor_scalar(b_rc[:, 6:7], b_stat[:, 15:16],
                                               GSCALE, None, ALU.mult))
                            A(eg.memset(b_rc[:, 3:4], 0.0))
                            A(eg.memset(b_rc[:, 7:8], 0.0))
                            A(eg.tensor_scalar(b_gf[:, 16:17], b_stat[:, 10:11],
                                               128.0, float(i * 1024),
                                               ALU.mult, ALU.add))
                            A(eg.tensor_tensor(b_gf[:, 16:17], b_gf[:, 16:17],
                                               b_stat[:, 11:12], ALU.add))
                            A(eg.tensor_scalar(b_gf[:, 17:18], b_stat[:, 13:14],
                                               128.0, float(i * 1024),
                                               ALU.mult, ALU.add))
                            A(eg.tensor_tensor(b_gf[:, 17:18], b_gf[:, 17:18],
                                               b_stat[:, 14:15], ALU.add))
                            A(eg.tensor_copy(b_rcu0, b_rc[:, 0:4]))
                            A(eg.tensor_copy(b_rcu1, b_rc[:, 4:8]))
                            A(eg.tensor_copy(b_sidx0, b_gf[:, 16:17]))
                            return eg.tensor_copy(b_sidx1, b_gf[:, 17:18])
                        cev = step("vector", [cu, chain_end.get(("sc", i - 1))],
                                   a4)
                        chain_end[i] = cev
                        sc0 = idma("sc", s_recs, bass.IndirectOffsetOnAxis(
                            ap=b_sidx0, axis=0), b_rcu0, None,
                            waits=[cev], bounds_check=NRECS - 1,
                            oob_is_err=False)
                        sc1 = idma("sc", s_recs, bass.IndirectOffsetOnAxis(
                            ap=b_sidx1, axis=0), b_rcu1, None,
                            bounds_check=NRECS - 1, oob_is_err=False)
                        chain_end[("sc", i)] = step("gpsimd", [sc1],
                                                    lambda eg: None)
                    gbar = [(s, s.val) for s in ALLSEMS]
                if t_g is not None:
                    dma("scalar", "tap", t_g, b_g, waits=gbar)
                    dma("scalar", "tap", t_G, b_G)
                    dma("scalar", "tap", t_cnt, b_cnt)
            return body

        if stop_after == 'V':
            return nc, dbg
        run_phase(phaseA)
        mid.close()

        b_rbrep = sbuf("b_rbrep", [128, NRUN], F32)
        b_b8rep = sbuf("b_b8rep", [128, E], F32)
        b_cc2 = sbuf("b_cc2", [E + 1, D], F32)

        def phaseR(ph):
            b_ta = sbuf("pr_ta", [1, NT * E], F32, ph)
            b_tb = sbuf("pr_tb", [1, NT * E], F32, ph)
            b_ne = sbuf("pr_ne", [1, E], F32, ph)
            b_e8a = sbuf("pr_e8a", [1, E], F32, ph)
            b_e8b = sbuf("pr_e8b", [1, E], F32, ph)
            b_eu = sbuf("pr_eu", [1, E], U32, ph)
            b_base = sbuf("pr_base", [1, E], F32, ph)
            b_trb = sbuf("pr_trb", [1, NRUN], F32, ph)
            b_trs = sbuf("pr_trs", [1, NRUN], F32, ph)
            b_tdl = sbuf("pr_tdl", [1, NRUN], F32, ph)
            b_trash = sbuf("pr_trash", [128, 4], U32, ph)
            p_rep = ph.enter_context(nc.psum_tensor("pr_rep", [128, 512], F32))[:]

            def body(he):
                own, dma, idma, step = he.own, he.dma, he.idma, he.step
                C = he.C
                cv = lambda a: a.rearrange("o (i e) -> o i e", e=E)

                def build(eg):
                    A = C
                    A(eg.reduce_sum(b_ne.rearrange("o (e u) -> o e u", u=1),
                                  b_cnt.rearrange("o (i e) -> o e i", e=E),
                                  axis=AX.X))
                    A(eg.tensor_scalar(b_e8a, b_ne, 255.0, 1.0 / 256.0,
                                     ALU.add, ALU.mult))
                    A(eg.tensor_copy(b_eu, b_e8a))
                    A(eg.tensor_copy(b_e8a, b_eu))
                    A(eg.tensor_scalar(b_e8a, b_e8a, 256.0, None, ALU.mult))
                    A(eg.tensor_copy(b_e8b[0:1, 0:1], b_e8a[0:1, 0:1]))
                    A(eg.tensor_tensor(b_e8b[0:1, 1:8], b_e8a[0:1, 1:8],
                                     b_e8a[0:1, 0:7], ALU.add))
                    A(eg.tensor_copy(b_e8a[0:1, 0:2], b_e8b[0:1, 0:2]))
                    A(eg.tensor_tensor(b_e8a[0:1, 2:8], b_e8b[0:1, 2:8],
                                     b_e8b[0:1, 0:6], ALU.add))
                    A(eg.tensor_copy(b_e8b[0:1, 0:4], b_e8a[0:1, 0:4]))
                    A(eg.tensor_tensor(b_e8b[0:1, 4:8], b_e8a[0:1, 4:8],
                                     b_e8a[0:1, 0:4], ALU.add))
                    A(eg.memset(b_base[0:1, 0:1], 0.0))
                    A(eg.tensor_copy(b_base[0:1, 1:8], b_e8b[0:1, 0:7]))
                    A(eg.tensor_copy(b_ta, b_cnt))
                    src, dst = b_ta, b_tb
                    for sh in (1, 2, 4, 8, 16):
                        A(eg.tensor_copy(cv(dst)[:, 0:sh, :], cv(src)[:, 0:sh, :]))
                        A(eg.tensor_tensor(cv(dst)[:, sh:NT, :], cv(src)[:, sh:NT, :],
                                         cv(src)[:, 0:NT - sh, :], ALU.add))
                        src, dst = dst, src
                    A(eg.memset(cv(dst)[:, 0:1, :], 0.0))
                    A(eg.tensor_copy(cv(dst)[:, 1:NT, :], cv(src)[:, 0:NT - 1, :]))
                    for i in range(NT):
                        A(eg.tensor_tensor(cv(dst)[:, i, :], cv(dst)[:, i, :],
                                         b_base, ALU.add))
                    for e in range(E):
                        A(eg.tensor_copy(
                            b_trb[0:1, e * (NT + 1):e * (NT + 1) + NT],
                            dst.rearrange("o (i e) -> o e i", e=E)[0:1, e, :]))
                    A(eg.tensor_tensor(b_e8a, b_base, b_ne, ALU.add))
                    A(eg.tensor_copy(
                        b_trb.rearrange("o (e k) -> o e k", k=NT + 1)[0:1, :, NT:NT + 1],
                        b_e8a.rearrange("o (e u) -> o e u", u=1)))
                    return eg.tensor_tensor(b_tdl, b_trs, b_trb, ALU.subtract)
                v0 = dma("sync", "cst2", b_trs, c_runstart)
                bv = step("vector", [v0], build)
                st1 = dma("sync", "rt", s_runtab[:, 0:1].rearrange("r o -> o r"),
                          b_trb, waits=[bv], allow_slow_non_contiguous=True)
                st2 = dma("sync", "rt", s_runtab[:, 1:2].rearrange("r o -> o r"),
                          b_tdl, allow_slow_non_contiguous=True)
                if t_runtab is not None:
                    dma("sync", "rt", t_runtab[:, 0:1].rearrange("r o -> o r"),
                        b_trb, allow_slow_non_contiguous=True)
                    dma("sync", "rt", t_runtab[:, 1:2].rearrange("r o -> o r"),
                        b_tdl, allow_slow_non_contiguous=True)
                for hf in range((NRUN + 511) // 512):
                    lo, hi = hf * 512, min(NRUN, (hf + 1) * 512)
                    mv = step("tensor", [bv, (VE, VE.val)], lambda eg, lo=lo, hi=hi:
                              eg.matmul(p_rep[:, 0:hi - lo], b_ones1f,
                                        b_trb[0:1, lo:hi], start=True, stop=True))
                    step("vector", [mv], lambda eg, lo=lo, hi=hi:
                         eg.tensor_copy(b_rbrep[:, lo:hi], p_rep[:, 0:hi - lo]))
                mv = step("tensor", [(VE, VE.val)], lambda eg: eg.matmul(
                    p_rep[:, 0:E], b_ones1f, b_base, start=True, stop=True))
                step("vector", [mv], lambda eg: eg.tensor_copy(
                    b_b8rep, p_rep[:, 0:E]))
                tv = step("vector", [], lambda eg: (
                    C(eg.memset(b_trash, 0)),
                    eg.memset(b_trash[:, 1:2], OOBD))[-1])
                for k in range(16):
                    off = NT * E * 128 + k * 128
                    dma("sync", "rt", s_recs[off:off + 128, :], b_trash,
                        waits=[tv])
            return body

        if stop_after == 'A':
            return nc, dbg
        run_phase(phaseR)

        def phaseF(ph):
            b_w1buf = sbuf("pf_w1", [128, NC * FF], BF16, ph)
            b_w2buf = sbuf("pf_w2", [128, NF * D], BF16, ph)
            b_h = sbuf("pf_h", [128, NF * 256], BF16, ph)
            b_xnPT = [sbuf(f"pf_xnpt{i}", [128, NC * 256], BF16, ph) for i in range(2)]
            b_xnP = [sbuf(f"pf_xnp{i}", [128, D], BF16, ph) for i in range(4)]
            b_slot = [sbuf(f"pf_slot{i}", [128, 1], F32, ph) for i in range(4)]
            b_ge = sbuf("pf_ge", [128, NRUN], F32, ph)
            b_ridx = [sbuf(f"pf_ridx{i}", [128, 1], U32, ph) for i in range(4)]
            b_rp = [sbuf(f"pf_rp{i}", [128, 2], F32, ph) for i in range(4)]
            b_rec = [sbuf(f"pf_rec{i}", [128, 4], U32, ph) for i in range(4)]
            b_gp = [sbuf(f"pf_gp{i}", [128, 1], F32, ph) for i in range(4)]
            b_widx = [sbuf(f"pf_widx{i}", [128, 1], U32, ph) for i in range(2)]
            b_eprev = sbuf("pf_eprev", [128, 1], F32, ph)
            b_ecur = sbuf("pf_ecur", [128, 1], F32, ph)
            b_big = sbuf("pf_big", [128, 1], F32, ph)
            b_tmp = sbuf("pf_tmp", [128, 2], F32, ph)
            b_b1c = [sbuf(f"pf_b1c{i}", [128, NF], F32, ph) for i in range(2)]
            b_y = [sbuf(f"pf_y{i}", [128, D], F32, ph) for i in range(2)]
            b_rb1 = sbuf("pf_rb1", [128, NF], BF16, ph)
            b_b2r = sbuf("pf_b2r", [1, D], F32, ph)
            b_ccstg = sbuf("pf_ccstg", [1, D], F32, ph)
            b_cestg = sbuf("pf_cestg", [1, 512], F32, ph)
            b_ctacc = sbuf("pf_ctacc", [1, D], F32, ph)
            p_h = [ph.enter_context(nc.psum_tensor(f"pf_h{i}", [128, 256], F32))[:]
                   for i in range(2)]
            p_o = [ph.enter_context(nc.psum_tensor(f"pf_o{i}", [128, 512], F32))[:]
                   for i in range(2)]
            p_tp2 = [ph.enter_context(
                nc.psum_tensor(f"pf_tp{i}", [128, 128], BF16))[:] for i in range(2)]
            p_ce = ph.enter_context(nc.psum_tensor("pf_ce", [1, 512], F32))[:]

            def body(he):
                own, dma, idma, step = he.own, he.dma, he.idma, he.step
                C = he.C
                eg = he.eg
                iv = step("vector", [], lambda eg: (
                    C(eg.memset(b_eprev, -1.0)),
                    C(eg.memset(b_big, 1.0e7)),
                    eg.memset(b_ctacc, 0.0))[-1])
                chain_v, rp_g, rec_g, xnp_g = {}, {}, {}, {}
                w_g, tp_m, tp_v = {}, {}, {}
                tplist = []
                mm1_m, rl_m, mm2_m, yev, sct = {}, {}, {}, {}, {}
                b1_g = {}

                def chain1(tau):
                    def f(eg, tau=tau):
                        C(eg.tensor_scalar(b_slot[tau % 4], b_iota,
                                           float(tau * 128), None, ALU.add))
                        C(eg.tensor_scalar(b_ge, b_rbrep, b_slot[tau % 4], None,
                                           ALU.is_le))
                        C(eg.reduce_sum(b_stat[:, 0:1], b_ge, axis=AX.X))
                        C(eg.tensor_scalar(b_stat[:, 1:2], b_stat[:, 0:1], -1.0,
                                           None, ALU.add))
                        return eg.tensor_copy(b_ridx[tau % 4], b_stat[:, 1:2])
                    chain_v[("s1", tau)] = step(
                        "vector", [rec_g.get(tau - 4)], f)
                    rp_g[("rp", tau)] = idma(
                        "rp", b_rp[tau % 4], None, s_runtab,
                        bass.IndirectOffsetOnAxis(ap=b_ridx[tau % 4], axis=0),
                        waits=[chain_v[("s1", tau)]],
                        bounds_check=NRUN - 1, oob_is_err=False)

                    def f2(eg, tau=tau):
                        C(eg.tensor_tensor(b_stat[:, 2:3], b_slot[tau % 4],
                                           b_rp[tau % 4][:, 1:2], ALU.add))
                        return eg.tensor_copy(b_ridx[tau % 4], b_stat[:, 2:3])
                    chain_v[("s3", tau)] = step(
                        "vector", [rp_g[("rp", tau)]], f2)
                    rec_g[tau] = idma(
                        "rc", b_rec[tau % 4], None, s_recs,
                        bass.IndirectOffsetOnAxis(ap=b_ridx[tau % 4], axis=0),
                        waits=[chain_v[("s3", tau)], sct.get(tau - 4)],
                        bounds_check=NRECS - 1, oob_is_err=False)
                    xnp_g[tau] = idma(
                        "xp%d" % (tau % 4), b_xnP[tau % 4], None, s_xn,
                        bass.IndirectOffsetOnAxis(ap=b_rec[tau % 4][:, 0:1], axis=0),
                        waits=[rec_g[tau], tp_m.get((tau - 4, NC - 1))],
                        bounds_check=T - 1, oob_is_err=False)

                    def f3(eg, tau=tau):
                        C(eg.tensor_copy(b_stat[:, 3:4], b_rec[tau % 4][:, 2:3]))
                        return eg.tensor_scalar(b_gp[tau % 4], b_stat[:, 3:4],
                                                1.0 / GSCALE, None, ALU.mult)
                    chain_v[("g", tau)] = step(
                        "vector", [rec_g[tau]], f3)

                def wfetch(pr):
                    tau = 2 * pr
                    def f(eg, tau=tau, pr=pr):
                        C(eg.tensor_scalar(b_ge[:, 0:E], b_b8rep,
                                           b_slot[tau % 4], None, ALU.is_le))
                        C(eg.reduce_sum(b_stat[:, 4:5], b_ge[:, 0:E], axis=AX.X))
                        C(eg.tensor_scalar(b_ecur, b_stat[:, 4:5], -1.0, None,
                                           ALU.add))
                        C(eg.tensor_tensor(b_stat[:, 5:6], b_ecur, b_eprev,
                                           ALU.not_equal))
                        C(eg.tensor_scalar(b_tmp[:, 0:1], b_ecur, 128.0, None,
                                           ALU.mult))
                        C(eg.tensor_tensor(b_tmp[:, 0:1], b_tmp[:, 0:1], b_iota,
                                           ALU.add))
                        C(eg.tensor_tensor(b_tmp[:, 0:1], b_tmp[:, 0:1], b_big,
                                           ALU.subtract))
                        C(eg.tensor_tensor(b_tmp[:, 0:1], b_tmp[:, 0:1],
                                           b_stat[:, 5:6], ALU.mult))
                        C(eg.tensor_tensor(b_tmp[:, 1:2], b_tmp[:, 0:1], b_big,
                                           ALU.add))
                        C(eg.tensor_copy(b_eprev, b_ecur))
                        return eg.tensor_copy(b_widx[pr % 2], b_tmp[:, 1:2])
                    chain_v[("w", pr)] = step(
                        "vector", [chain_v[("s1", tau)],
                                   w_g.get(("w2b", pr - 2))], f)
                    wwait = [chain_v[("w", pr)],
                             mm2_m.get((2 * pr - 1, 1))]
                    w_g[("w1a", pr)] = idma(
                        "wg", b_w1buf[:, 0:16384], None, s_w1s,
                        bass.IndirectOffsetOnAxis(ap=b_widx[pr % 2], axis=0),
                        waits=wwait, bounds_check=E * 128 - 1, oob_is_err=False)
                    w_g[("w1b", pr)] = idma(
                        "wg", b_w1buf[:, 16384:32768], None, s_w1s,
                        bass.IndirectOffsetOnAxis(ap=b_widx[pr % 2], axis=0),
                        element_offset=16384,
                        bounds_check=E * 128 - 1, oob_is_err=False)
                    b1_g[pr] = idma(
                        "wg", b_b1c[0], None, s_b1T,
                        bass.IndirectOffsetOnAxis(ap=b_widx[pr % 2], axis=0),
                        bounds_check=E * 128 - 1, oob_is_err=False)
                    w_g[("w2a", pr)] = idma(
                        "wg", b_w2buf[:, 0:16384], None, s_w2s,
                        bass.IndirectOffsetOnAxis(ap=b_widx[pr % 2], axis=0),
                        bounds_check=E * 128 - 1, oob_is_err=False)
                    w_g[("w2b", pr)] = idma(
                        "wg", b_w2buf[:, 16384:32768], None, s_w2s,
                        bass.IndirectOffsetOnAxis(ap=b_widx[pr % 2], axis=0),
                        element_offset=16384,
                        bounds_check=E * 128 - 1, oob_is_err=False)

                chain1(0)
                chain1(1)
                chain1(2)
                chain1(3)
                wfetch(0)
                for pr in range(NPAIR):
                    for tt in range(2):
                        tau = 2 * pr + tt
                        for jc in range(NC):
                            sl = len(tplist) % 2
                            w = [xnp_g[tau],
                                 tplist[-2] if len(tplist) >= 2 else None]
                            mv = step("tensor", w, lambda eg, jc=jc, tau=tau, sl=sl:
                                      eg.transpose(
                                          p_tp2[sl],
                                          b_xnP[tau % 4][:, jc * 128:(jc + 1) * 128],
                                          b_identb))
                            tp_m[(tau, jc)] = mv
                            tp_v[(tau, jc)] = step(
                                "vector", [mv,
                                           mm1_m.get((tau // 2 - 2, NF - 1))],
                                lambda eg, jc=jc, tt=tt, tau=tau, sl=sl:
                                eg.tensor_copy(
                                    b_xnPT[(tau // 2) % 2][:, jc * 256 + tt * 128:
                                                           jc * 256 + (tt + 1) * 128],
                                    p_tp2[sl]))
                            tplist.append(tp_v[(tau, jc)])
                    xptr = [tp_v[(2 * pr + tt, jc)]
                            for tt in range(2) for jc in range(NC)]
                    for Fc in range(NF):
                        w = [w_g[("w2b", pr)], rl_m.get((pr, Fc - 2))]
                        if Fc == 0:
                            w += xptr

                        def mm1(eg, pr=pr, Fc=Fc):
                            r = None
                            for jc in range(NC):
                                r = eg.matmul(
                                    p_h[Fc % 2],
                                    b_w1buf[:, jc * FF + Fc * 128:
                                            jc * FF + (Fc + 1) * 128],
                                    b_xnPT[pr % 2][:, jc * 256:(jc + 1) * 256],
                                    start=(jc == 0), stop=(jc == NC - 1))
                            return r
                        mm1_m[(pr, Fc)] = step("tensor", w, mm1)
                        rl_m[(pr, Fc)] = step(
                            "scalar", [mm1_m[(pr, Fc)], w_g[("w2b", pr)]],
                            lambda eg, pr=pr, Fc=Fc: eg.activation(
                                b_h[:, Fc * 256:(Fc + 1) * 256], p_h[Fc % 2],
                                AF.Relu, bias=b_b1c[0][:, Fc:Fc + 1]))
                    for tt in range(2):
                        tau = 2 * pr + tt
                        for hf in range(2):
                            k = (tt * 2 + hf) % 2
                            w = [rl_m[(pr, NF - 1)], w_g[("w2b", pr)],
                                 yev.get((tau - 1, hf))]

                            def mm2(eg, tt=tt, hf=hf, k=k):
                                r = None
                                for Fc in range(NF):
                                    r = eg.matmul(
                                        p_o[k],
                                        b_h[:, Fc * 256 + tt * 128:
                                            Fc * 256 + (tt + 1) * 128],
                                        b_w2buf[:, Fc * D + hf * 512:
                                                Fc * D + (hf + 1) * 512],
                                        start=(Fc == 0), stop=(Fc == NF - 1))
                                return r
                            mm2_m[(tau, hf)] = step("tensor", w, mm2)
                            yev[(tau, hf)] = step(
                                "vector", [mm2_m[(tau, hf)],
                                           sct.get(tau - 2),
                                           chain_v[("g", tau)]],
                                lambda eg, tau=tau, hf=hf, k=k: eg.tensor_scalar(
                                    b_y[tau % 2][:, hf * 512:(hf + 1) * 512],
                                    p_o[k], b_gp[tau % 4], None, ALU.mult))
                        sct[tau] = idma(
                            "st%d" % (tau % 2), s_pairs, bass.IndirectOffsetOnAxis(
                                ap=b_rec[tau % 4][:, 1:2], axis=0),
                            b_y[tau % 2], None,
                            waits=[yev[(tau, 1)]],
                            bounds_check=2 * T - 1, oob_is_err=False)
                    if pr + 1 < NPAIR:
                        wfetch(pr + 1)
                    if pr + 2 < NPAIR:
                        chain1(2 * pr + 4)
                        chain1(2 * pr + 5)
                for e in range(E):
                    wv2 = dma("sync", "ce", b_w2buf, s_w2s[e * 128:(e + 1) * 128, :],
                              waits=[(MM, MM.val)])
                    bv1 = dma("sync", "ce", b_b1c[0], s_b1T[e * 128:(e + 1) * 128, :],
                              waits=[(ACS, ACS.val)])
                    bv2 = dma("sync", "ce", b_b2r,
                              db2[e].rearrange("(o d) -> o d", o=1),
                              waits=[(VE, VE.val)])
                    r1 = step("scalar", [bv2], lambda eg: eg.activation(
                        b_rb1, b_b1c[0], AF.Relu))
                    for hf in range(2):
                        def cemm(eg, hf=hf):
                            r = None
                            for Fc in range(NF):
                                r = eg.matmul(
                                    p_ce, b_rb1[:, Fc:Fc + 1],
                                    b_w2buf[:, Fc * D + hf * 512:
                                            Fc * D + (hf + 1) * 512],
                                    start=(Fc == 0), stop=(Fc == NF - 1))
                            return r
                        mv = step("tensor", [r1, (VE, VE.val)], cemm)
                        step("vector", [mv],
                             lambda eg, hf=hf: (
                                 C(eg.tensor_copy(b_cestg, p_ce)),
                                 C(eg.scalar_tensor_tensor(
                                     b_ccstg[0:1, hf * 512:(hf + 1) * 512],
                                     b_b2r[0:1, hf * 512:(hf + 1) * 512], 1.0,
                                     p_ce, ALU.mult, ALU.subtract)),
                                 eg.tensor_tensor(
                                     b_ctacc[0:1, hf * 512:(hf + 1) * 512],
                                     b_ctacc[0:1, hf * 512:(hf + 1) * 512],
                                     b_cestg, ALU.add))[-1])
                    dma("scalar", "ce", s_cc[e].rearrange("(o d) -> o d", o=1),
                        b_ccstg, waits=[(VE, VE.val)])
                dma("scalar", "ce", s_cc[E].rearrange("(o d) -> o d", o=1), b_ctacc,
                    waits=[(VE, VE.val)])
            return body

        if stop_after == 'R':
            return nc, dbg
        run_phase(phaseF)


        def phaseC(ph):
            b_cb = [sbuf(f"pc_cb{i}", [128, D], F32, ph) for i in range(2)]
            b_p1 = [sbuf(f"pc_p1{i}", [128, D], F32, ph) for i in range(2)]
            b_l2 = [sbuf(f"pc_l2{i}", [128, D], F32, ph) for i in range(2)]
            b_x2 = [sbuf(f"pc_x2{i}", [128, D], F32, ph) for i in range(2)]
            b_f1 = [sbuf(f"pc_f1{i}", [128, D], F32, ph) for i in range(2)]
            b_f2 = sbuf("pc_f2", [128, D], F32, ph)
            b_fo = [sbuf(f"pc_fo{i}", [128, D], F32, ph) for i in range(2)]
            b_g9 = sbuf("pc_g9", [128, 128], F32, ph)
            b_g9T = sbuf("pc_g9T", [128, 128], F32, ph)
            b_junk = sbuf("pc_junk", [128, D], F32, ph)
            p_g9 = ph.enter_context(nc.psum_tensor("pc_pg9", [128, 128], F32))[:]
            p_c = [ph.enter_context(nc.psum_tensor(f"pc_c{i}", [128, 512], F32))[:]
                   for i in range(2)]

            def body(he):
                own, dma, idma, step = he.own, he.dma, he.idma, he.step
                C = he.C
                ccv = dma("sync", "ccl", b_cc2, s_cc)
                if t_cc is not None:
                    dma("sync", "taps", t_cc, s_cc)
                if t_pairs is not None:
                    for t in range(2 * NT):
                        dma("sync", "taps", t_pairs[t * 128:(t + 1) * 128, :],
                            s_pairs[t * 128:(t + 1) * 128, :])
                ld_m, st_m, fdone = {}, {}, {}
                g9done = {}
                corr_ev = []
                for t in range(NT):
                    pl = "pl%d" % (t % 2)
                    l1 = dma("sync", pl, b_cb[t % 2],
                             s_pairs[t * 128:(t + 1) * 128, :],
                             waits=[fdone.get(t - 2)])
                    l2 = dma("sync", pl, b_p1[t % 2],
                             s_pairs[(T + t * 128):(T + (t + 1) * 128), :])
                    l3 = dma("sync", pl, b_l2[t % 2], s_ln2[t])
                    l4 = dma("sync", pl, b_x2[t % 2], s_x2[t])
                    gv = step("vector", [g9done.get(t - 1)],
                              lambda eg, t=t: (
                                  C(eg.tensor_copy(b_g9[:, 0:8],
                                                   b_g[:, t * 8:(t + 1) * 8])),
                                  C(eg.memset(b_g9[:, 9:16], 0.0)),
                                  eg.tensor_copy(b_g9[:, 8:9], b_G[:, t:t + 1]))[-1])
                    tv = step("tensor", [gv], lambda eg: eg.transpose(
                        p_g9, b_g9, b_identf))
                    ge = step("vector", [tv], lambda eg: eg.tensor_copy(
                        b_g9T[0:16, :], p_g9[0:16, :]))
                    for hf in range(2):
                        w = [ge, ccv]
                        if len(corr_ev) >= 2:
                            w.append(corr_ev[-2])
                        mv = step("tensor", w, lambda eg, hf=hf: eg.matmul(
                            p_c[hf], b_g9T[0:9, :],
                            b_cc2[0:9, hf * 512:(hf + 1) * 512],
                            start=True, stop=True))
                        g9done[t] = mv
                        ev = step("vector", [mv, l4],
                                  lambda eg, t=t, hf=hf: (
                                      C(eg.tensor_tensor(
                                          b_f1[t % 2][:, hf * 512:(hf + 1) * 512],
                                          b_cb[t % 2][:, hf * 512:(hf + 1) * 512],
                                          b_p1[t % 2][:, hf * 512:(hf + 1) * 512],
                                          ALU.add)),
                                      C(eg.tensor_tensor(
                                          b_f1[t % 2][:, hf * 512:(hf + 1) * 512],
                                          b_f1[t % 2][:, hf * 512:(hf + 1) * 512],
                                          b_l2[t % 2][:, hf * 512:(hf + 1) * 512],
                                          ALU.add)),
                                      eg.scalar_tensor_tensor(
                                          b_f1[t % 2][:, hf * 512:(hf + 1) * 512],
                                          p_c[hf], 1.0,
                                          b_f1[t % 2][:, hf * 512:(hf + 1) * 512],
                                          ALU.mult, ALU.add,
                                          accum_out=b_stat[:, 8 + hf:9 + hf]))[-1])
                        corr_ev.append(ev)
                    sq = step("vector", [corr_ev[-1]], lambda eg, t=t: (
                        C(eg.tensor_tensor(b_stat[:, 0:1], b_stat[:, 8:9],
                                           b_stat[:, 9:10], ALU.add)),
                        eg.scalar_tensor_tensor(
                            b_junk, b_f1[t % 2], 1.0, b_f1[t % 2],
                            ALU.mult, ALU.mult, accum_out=b_stat[:, 1:2]))[-1])
                    y2 = ln_seq(he, b_f1[t % 2], b_f2, None, waits=[sq],
                                sums_precomputed=True)
                    fdone[t] = step("vector", [y2, st_m.get(t - 2)],
                                    lambda eg, t=t: eg.tensor_tensor(
                                        b_fo[t % 2], b_f2, b_x2[t % 2], ALU.add))
                    st_m[t] = dma("scalar", "ou%d" % (t % 2),
                                  dout[t * 128:(t + 1) * 128, :],
                                  b_fo[t % 2], waits=[fdone[t]])
            return body

        if stop_after == 'F':
            return nc, dbg
        run_phase(phaseC)

    return nc, dbg


def _shard_inputs(inputs):
    x = np.ascontiguousarray(np.asarray(inputs["L0_attn_x"], dtype=np.float32))
    names = dict(
        Wq="L0_attn_Wq", bq="L0_attn_bq", Wk="L0_attn_Wk", bk="L0_attn_bk",
        Wv="L0_attn_Wv", bv="L0_attn_bv", Wo="L0_attn_Wo", bo="L0_attn_bo",
        gamma="L0_ffn_gamma", beta="L0_ffn_beta", Wr="L0_ffn_Wr", br="L0_ffn_br",
        W1="L0_ffn_W1", b1="L0_ffn_b1", W2="L0_ffn_W2", b2="L0_ffn_b2")
    shared = {k: np.ascontiguousarray(np.asarray(inputs[v], dtype=np.float32))
              for k, v in names.items()}
    return [dict(x=x[b], **shared) for b in range(8)]


def kernel(**inputs):
    in_maps = _shard_inputs(inputs)
    nc, _ = build_program()
    res = run_bass_kernel_spmd(nc, in_maps, core_ids=list(range(8)))
    return np.stack([r["out"] for r in res.results], axis=0)


# version 05
# speedup vs baseline: 1.0047x; 1.0047x over previous
import contextlib
import numpy as np
import concourse.bass as bass
import concourse.mybir as mybir
from concourse.bass_utils import run_bass_kernel_spmd

F32 = mybir.dt.float32
BF16 = mybir.dt.bfloat16
U32 = mybir.dt.uint32
AF = mybir.ActivationFunctionType
ALU = mybir.AluOpType
AX = mybir.AxisListType

T, D, H, E, FF = 4096, 1024, 16, 8, 4096
NT, NC, NF = T // 128, D // 128, FF // 128
NG = NT // 4
CLIST = [0, 1, 2, 4, 8, 16]
KWIN = 20
LN_EPS = 1e-5
INV_D = 1.0 / D
NRUN = NT * E + E
NRECS = NT * E * 128 + 2048
NPTILE = 2 * T // 128 + 2 * E
NPAIR = NPTILE // 2
GSCALE = float(1 << 24)
OOBD = 1 << 20


def _block_masks():
    offs = {1 << k for k in range(12)}
    ms = []
    for c in CLIST:
        m = np.zeros((128, 128), dtype=np.float32)
        for a in range(128):
            for b in range(128):
                d = a - b + 128 * c
                if d in offs or (d == 0 and c == 0):
                    m[a, b] = 1.0
        ms.append(m.T.copy())
    return np.stack(ms)


def build_program(debug_taps=(), stop_after=None, p0_cut=99):
    nc = bass.Bass("TRN2", target_bir_lowering=False, debug=False)

    dx = nc.dram_tensor("x", [T, D], F32, kind="ExternalInput").ap()
    dW = {w: nc.dram_tensor("W" + w, [D, D], F32, kind="ExternalInput").ap()
          for w in ("q", "k", "v", "o")}
    db = {w: nc.dram_tensor("b" + w, [D], F32, kind="ExternalInput").ap()
          for w in ("q", "k", "v", "o")}
    dgam = nc.dram_tensor("gamma", [D], F32, kind="ExternalInput").ap()
    dbet = nc.dram_tensor("beta", [D], F32, kind="ExternalInput").ap()
    dWr = nc.dram_tensor("Wr", [D, E], F32, kind="ExternalInput").ap()
    dbr = nc.dram_tensor("br", [E], F32, kind="ExternalInput").ap()
    dW1 = nc.dram_tensor("W1", [E, D, FF], F32, kind="ExternalInput").ap()
    db1 = nc.dram_tensor("b1", [E, FF], F32, kind="ExternalInput").ap()
    dW2 = nc.dram_tensor("W2", [E, FF, D], F32, kind="ExternalInput").ap()
    db2 = nc.dram_tensor("b2", [E, D], F32, kind="ExternalInput").ap()
    dout = nc.dram_tensor("out", [T, D], F32, kind="ExternalOutput").ap()

    s_ln1 = nc.dram_tensor("s_ln1", [NT, 128, D], BF16, kind="Internal").ap()
    s_vext = nc.dram_tensor("s_vext", [NT, 128, H * 65], BF16, kind="Internal").ap()
    s_x2 = nc.dram_tensor("s_x2", [NT, 128, D], F32, kind="Internal").ap()
    s_ln2 = nc.dram_tensor("s_ln2", [NT, 128, D], F32, kind="Internal").ap()
    s_xn = nc.dram_tensor("s_xn", [T, D], BF16, kind="Internal").ap()
    s_wqkT = nc.dram_tensor("s_wqkT", [2, NC, 128, D], BF16, kind="Internal").ap()
    s_wvoT = nc.dram_tensor("s_wvoT", [2, NC, 128, D], BF16, kind="Internal").ap()
    s_w1s = nc.dram_tensor("s_w1s", [E * 128, NC * FF], BF16, kind="Internal").ap()
    s_w2s = nc.dram_tensor("s_w2s", [E * 128, NF * D], BF16, kind="Internal").ap()
    s_b1T = nc.dram_tensor("s_b1T", [E * 128, NF], F32, kind="Internal").ap()
    s_cc = nc.dram_tensor("s_cc", [E + 1, D], F32, kind="Internal").ap()
    s_recs = nc.dram_tensor("s_recs", [NRECS, 4], U32, kind="Internal").ap()
    s_runtab = nc.dram_tensor("s_runtab", [NRUN, 2], F32, kind="Internal").ap()
    s_adj = nc.dram_tensor("s_adj", [NT, H * 65], F32, kind="Internal").ap()
    s_tcs = nc.dram_tensor("s_tcs", [NT, H * 65], F32, kind="Internal").ap()
    s_pairs = nc.dram_tensor("s_pairs", [2 * T, D], F32, kind="Internal").ap()

    c_masks = nc.inline_tensor(_block_masks(), name="cmasks").ap()
    c_ident = nc.inline_tensor(np.eye(128, dtype=np.float32), name="cident").ap()
    c_nego = nc.inline_tensor(-np.ones((128, 128), np.float32), name="cnego").ap()
    c_ones1 = nc.inline_tensor(np.ones((1, 128), np.float32), name="cones1").ap()
    c_LT = nc.inline_tensor(np.triu(np.ones((128, 128), np.float32), 1), name="cLT").ap()
    c_U8 = nc.inline_tensor(np.triu(np.ones((8, 8), np.float32), 1), name="cU8").ap()
    c_iota = nc.inline_tensor(np.arange(128, dtype=np.float32).reshape(128, 1),
                              name="ciota").ap()
    c_evec = nc.inline_tensor(
        np.tile(np.arange(8, dtype=np.float32), (128, 1)), name="cevec").ap()
    rs = np.zeros((1, NRUN), np.float32)
    for e in range(E):
        for i in range(NT):
            rs[0, e * (NT + 1) + i] = (i * E + e) * 128
        rs[0, e * (NT + 1) + NT] = NT * E * 128
    c_runstart = nc.inline_tensor(rs, name="crunstart").ap()
    _sm = np.ones((NT, NT), np.float32)
    for _i in range(NT):
        for _c in CLIST:
            if _i >= _c:
                _sm[_i - _c, _i] = 0.0
    c_S32 = nc.inline_tensor(_sm, name="cS32").ap()

    dbg = {}

    def tap(name, shape, dtype=F32):
        if name in debug_taps:
            dbg[name] = nc.dram_tensor("dbg_" + name, shape, dtype,
                                       kind="ExternalOutput").ap()
        return dbg.get(name)

    t_ln1 = tap("ln1", [NT, 128, D], BF16)
    t_vext = tap("vext", [NT, 128, H * 65], BF16)
    t_vsum = tap("vsum", [1, H * 65])
    t_ao = tap("ao", [NT, 128, D], BF16)
    t_x2 = tap("x2", [NT, 128, D])
    t_ln2 = tap("ln2", [NT, 128, D])
    t_g = tap("g", [128, NT * E])
    t_G = tap("G", [128, NT])
    t_cnt = tap("cnt", [1, NT * E])
    t_runtab = tap("runtab", [NRUN, 2])
    t_cc = tap("cc", [E + 1, D])
    t_pairs = tap("pairs", [2 * T, D])

    ctx = contextlib.ExitStack()
    with ctx:
        EE = ctx.enter_context

        class Sem:
            def __init__(self, name):
                self.h = EE(nc.semaphore(name))
                self.val = 0

        STREAM_NAMES = [
            "cst", "row", "wstg", "wst", "wv8", "x0", "x1", "stl0", "stl1",
            "stv0", "stv1", "tap", "lg0", "lg1", "vr", "ws0", "ws1", "wo",
            "wc", "sx0", "sx1", "sl20", "sl21", "sxn0", "sxn1", "ta0", "ta1",
            "sc", "cst2", "rt", "rp", "rc", "xp0", "xp1", "xp2", "xp3", "wg",
            "st0", "st1", "ce", "pl0", "pl1", "ccl", "ou0", "ou1", "cstg",
            "taps", "aj0", "aj1", "tc0", "tc1", "vr0", "vr1"]
        STREAMS = {n: Sem("q_" + n) for n in STREAM_NAMES}
        MM, VE, ACS, POG, SYN, BAR = (Sem("mm"), Sem("ve"), Sem("ac"), Sem("po"),
                                      Sem("sy"), Sem("bar"))
        CHPOOL = [{e: EE(nc.semaphore(f"ch{p}_{e}"))
                   for e in ("sync", "gpsimd", "tensor", "vector", "scalar")}
                  for p in range(8)]
        ALLSEMS = list(STREAMS.values()) + [MM, VE, ACS, POG, SYN]
        MSEM = {"tensor": MM, "vector": VE, "scalar": ACS, "gpsimd": POG,
                "sync": SYN}

        def sbuf(name, shape, dtype, st=None):
            return (st or ctx).enter_context(nc.sbuf_tensor(name, shape, dtype))[:]

        b_identf = sbuf("b_identf", [128, 128], F32)
        b_eps = sbuf("b_eps", [128, 1], F32)
        b_identb = sbuf("b_identb", [128, 128], BF16)
        b_ones1b = sbuf("b_ones1b", [1, 128], BF16)
        b_ones1f = sbuf("b_ones1f", [1, 128], F32)
        b_onescolb = sbuf("b_onescolb", [128, 1], BF16)
        b_onescolf = sbuf("b_onescolf", [128, 1], F32)
        b_iota = sbuf("b_iota", [128, 1], F32)
        b_wr = sbuf("b_wr", [128, NC * E], BF16)
        b_brrep = sbuf("b_brrep", [128, E], F32)
        b_g = sbuf("b_g", [128, NT * E], F32)
        b_G = sbuf("b_G", [128, NT], F32)
        b_cnt = sbuf("b_cnt", [1, NT * E], F32)
        b_stat = sbuf("b_stat", [128, 16], F32)
        b_r8 = sbuf("b_r8", [128, E], F32)
        b_r8b = sbuf("b_r8b", [128, E], F32)
        b_r8c = sbuf("b_r8c", [128, E], F32)

        block = EE(nc.Block())
        ENGINES = ["sync", "gpsimd", "tensor", "vector", "scalar"]
        phase_no = [0]

        def run_phase(builder):
            with contextlib.ExitStack() as ph:
                body = builder(ph)
                snap = [s.val for s in ALLSEMS]

                def walk(en, eg):
                    for s, v in zip(ALLSEMS, snap):
                        s.val = v
                    eg.wait_ge(BAR.h, 5 * phase_no[0])
                    issued = {}

                    class He:
                        pass

                    he = He()
                    he.en = en
                    he.eg = eg
                    wm = {}

                    def own(name):
                        return en == name

                    def do_waits(waits):
                        for wv in waits:
                            if not wv:
                                continue
                            s, v = wv
                            if v > 0 and v > wm.get(id(s), -1):
                                eg.wait_ge(s.h, v)
                                wm[id(s)] = v

                    def dma(queue, stream, out, in_, waits=(), **kw):
                        sem = STREAMS[stream]
                        if en == queue:
                            do_waits(waits)
                            eg.dma_start(out, in_, **kw).then_inc(sem.h, 16)
                        sem.val += 16
                        issued[(queue, stream)] = sem.val
                        return (sem, sem.val)

                    bc_regs = {}

                    def idma(stream, out, out_off, in_, in_off, waits=(), **kw):
                        sem = STREAMS[stream]
                        if en == "gpsimd":
                            do_waits(waits)
                            bc = kw.get("bounds_check")
                            if isinstance(bc, int):
                                if bc not in bc_regs:
                                    bc_regs[bc] = eg.to_reg(bc)
                                kw["bounds_check"] = bc_regs[bc]
                            eg.indirect_dma_start(
                                out, out_off, in_, in_off, **kw).then_inc(sem.h, 16)
                        sem.val += 16
                        issued[("gpsimd", stream)] = sem.val
                        return (sem, sem.val)

                    chsem = CHPOOL[phase_no[0]][en]
                    chv = [0]

                    def C(inst):
                        chv[0] += 1
                        inst.then_inc(chsem, 1)
                        eg.wait_ge(chsem, chv[0])
                        return inst
                    he.C = C

                    def step(engine, waits, fn):
                        sem = MSEM[engine]
                        if en == engine:
                            do_waits(waits)
                            if sem.val > 0:
                                eg.wait_ge(sem.h, sem.val)
                            r = fn(eg)
                            insts = list(r) if isinstance(r, (list, tuple)) else [r]
                            insts = [i for i in insts if i is not None]
                            if not insts:
                                insts = [eg.nop()]
                            assert len(insts) == 1, "chain multi-op fns with he.C"
                            insts[-1].then_inc(sem.h, 1)
                        sem.val += 1
                        return (sem, sem.val)
                    he.do_waits = do_waits

                    he.own, he.dma, he.idma, he.step = own, dma, idma, step
                    body(he)
                    for (q, stream), v in issued.items():
                        if en == q:
                            eg.wait_ge(STREAMS[stream].h, v)
                    eg.nop().then_inc(BAR.h, 1)

                for en in ENGINES:
                    getattr(block, en)(lambda eg, en=en: walk(en, eg))
                BAR.val += 5
                phase_no[0] += 1

        def phase0(ph):
            b_wstg = sbuf("p0_wstg", [128, D], BF16, ph)
            b_wstg2 = sbuf("p0_wstg2", [128, D], BF16, ph)
            b_jstg = [sbuf(f"p0_jstg{j}", [128, D], BF16, ph) for j in range(NC)]
            b_row1 = sbuf("p0_row1", [1, D], F32, ph)
            b_b1s = sbuf("p0_b1s", [128, 128], F32, ph)
            b_b1t = sbuf("p0_b1t", [128, 32], F32, ph)
            p_rep = ph.enter_context(nc.psum_tensor("p0_rep", [128, 512], F32))[:]
            p_tp2 = [ph.enter_context(
                nc.psum_tensor(f"p0_tp{i}", [128, 128], BF16))[:] for i in range(2)]
            p_tpf = ph.enter_context(nc.psum_tensor("p0_tpf", [128, 128], F32))[:]

            def body(he):
                own, dma, idma, step = he.own, he.dma, he.idma, he.step
                C = he.C
                dma("sync", "cst", b_masks.rearrange("p (c l) -> p c l", l=128),
                    c_masks.rearrange("c m l -> m c l"))
                dma("gpsimd", "cstg", b_identb, c_ident)
                dma("gpsimd", "cstg", b_ones1b, c_ones1)
                dma("sync", "cst", b_ones1f, c_ones1)
                dma("sync", "cst", b_onescolf, c_ones1.rearrange("o p -> p o"),
                    allow_slow_non_contiguous=True)
                dma("sync", "cst", b_iota, c_iota)
                dma("sync", "cst", b_evec, c_evec)
                dma("sync", "cst", b_LT, c_LT)
                dma("sync", "cst", b_U8, c_U8)
                dma("sync", "cst", b_identf, c_ident)
                dma("sync", "cst", b_bq, db["q"].rearrange("(c p) -> p c", p=128),
                    allow_slow_non_contiguous=True)
                dma("sync", "cst", b_bk, db["k"].rearrange("(c p) -> p c", p=128),
                    allow_slow_non_contiguous=True)
                cstg_all = dma("gpsimd", "cstg",
                               b_wr.rearrange("p (c e) -> p c e", e=E),
                               dWr.rearrange("(c p) e -> p c e", p=128),
                               allow_slow_non_contiguous=True)
                cst_all = (STREAMS["cst"], STREAMS["cst"].val)
                step("gpsimd", [], lambda eg: (
                    C(eg.memset(b_onescolb, 1.0)),
                    eg.memset(b_eps, LN_EPS))[-1])
                if p0_cut < 2:
                    return

                for rn, src, dst in [
                        ("bv", db["v"], ROWS["bv"]), ("bo", db["o"], ROWS["bo"]),
                        ("gam", dgam, ROWS["gam"]), ("bet", dbet, ROWS["bet"])]:
                    v = dma("sync", "row", b_row1, src.rearrange("(o d) -> o d", o=1),
                            waits=[(MM, MM.val)])
                    for hf in range(2):
                        mv = step("tensor", [v, cst_all, cstg_all],
                                  lambda eg, hf=hf: eg.matmul(
                            p_rep, b_ones1f, b_row1[0:1, hf * 512:(hf + 1) * 512],
                            start=True, stop=True))
                        ev = step("vector", [mv], lambda eg, hf=hf, dst=dst:
                                  eg.tensor_copy(dst[:, hf * 512:(hf + 1) * 512], p_rep))
                        step("tensor", [ev], lambda eg: None)
                v = dma("sync", "row", b_row1[0:1, 0:E],
                        dbr.rearrange("(o e) -> o e", o=1), waits=[(MM, MM.val)])
                mv = step("tensor", [v], lambda eg: eg.matmul(
                    p_rep[:, 0:E], b_ones1f, b_row1[0:1, 0:E], start=True, stop=True))
                ev = step("vector", [mv], lambda eg:
                          eg.tensor_copy(b_brrep, p_rep[:, 0:E]))
                step("tensor", [ev], lambda eg: None)

                if p0_cut < 3:
                    return
                nweights = min(4, p0_cut - 30) if 30 <= p0_cut < 40 else \
                    (1 if 20 <= p0_cut < 30 else 4)
                jstg_st = [None]
                evs = []
                prev_st = None
                for wname, dst, major in [("q", s_wqkT[0], "oc"), ("k", s_wqkT[1], "oc"),
                                          ("v", s_wvoT[0], "jc"), ("o", s_wvoT[1], "jc")][:nweights]:
                    for oc in range(NC):
                        v = dma("gpsimd", "wstg",
                                b_wstg.rearrange("p (s f) -> p s f", f=512),
                                dW[wname][oc * 128:(oc + 1) * 128, :]
                                .rearrange("p (s f) -> p s f", f=512),
                                waits=[(MM, MM.val)])
                        if p0_cut == 21:
                            continue
                        for jc in range(NC):
                            sl = len(evs) % 2
                            w = [v]
                            if len(evs) >= 2:
                                w.append(evs[-2])
                            mv = step("tensor", w, lambda eg, sl=sl, jc=jc: eg.transpose(
                                p_tp2[sl], b_wstg[:, jc * 128:(jc + 1) * 128],
                                b_identb))
                            if p0_cut == 22:
                                evs.append(mv)
                                continue
                            if major == "oc":
                                dcb, dcc = b_wstg2, jc
                            else:
                                dcb, dcc = b_jstg[jc], oc
                            w2 = [mv]
                            if major == "oc" and jc == 0:
                                w2.append(prev_st)
                            if major == "jc" and oc == 0 and jc == 0:
                                w2.append(jstg_st[0])
                            ev = step("vector", w2, lambda eg, dcb=dcb, dcc=dcc, sl=sl:
                                      eg.tensor_copy(dcb[:, dcc * 128:(dcc + 1) * 128],
                                                     p_tp2[sl]))
                            evs.append(ev)
                        if major == "oc" and p0_cut != 23 and p0_cut != 22 \
                                and p0_cut != 21:
                            prev_st = dma("scalar", "wst", dst[oc], b_wstg2,
                                          waits=[evs[-1]])
                    if major == "jc":
                        for jc in range(NC):
                            jstg_st[0] = dma("scalar", "wst", dst[jc], b_jstg[jc],
                                             waits=[evs[-1]])

                if p0_cut < 4 or (20 <= p0_cut < 40):
                    return
                step("vector", [], lambda eg: eg.memset(b_b1s, 0.0))
                prev_st = None
                for e in range(E):
                    v = dma("sync", "row", b_b1s[0:32, :],
                            db1[e].rearrange("(c p) -> c p", p=128),
                            waits=[(VE, VE.val), prev_st, (MM, MM.val)])
                    mv = step("tensor", [v, cst_all], lambda eg: eg.transpose(
                        p_tpf, b_b1s, b_identf))
                    ev = step("vector", [mv], lambda eg:
                              eg.tensor_copy(b_b1t, p_tpf[:, 0:32]))
                    prev_st = dma("scalar", "wst", s_b1T[e * 128:(e + 1) * 128, :],
                                  b_b1t, waits=[ev])

            return body

        mid = ctx.enter_context(contextlib.ExitStack())
        b_masks = sbuf("b_masks", [128, 6 * 128], F32, mid)
        b_evec = sbuf("b_evec", [128, 8], F32, mid)
        b_LT = sbuf("b_LT", [128, 128], F32, mid)
        b_U8 = sbuf("b_U8", [8, 8], F32, mid)
        b_bq = sbuf("b_bq", [128, NC], F32, mid)
        b_bk = sbuf("b_bk", [128, NC], F32, mid)
        ROWS = {r: sbuf("b_row_" + r, [128, D], F32, mid)
                for r in ("bv", "bo", "gam", "bet")}

        def phaseNul(ph):
            def body(he):
                pass
            return body

        if stop_after == "nul":
            run_phase(phaseNul)
            return nc, dbg
        run_phase(phase0)

        def ln_seq(he, src, dst, sq_junk, waits=(), sums_precomputed=False):
            step, C = he.step, he.C

            def stats(eg):
                if not sums_precomputed:
                    C(eg.reduce_sum(b_stat[:, 0:1], src, axis=AX.X))
                    C(eg.scalar_tensor_tensor(
                        sq_junk, src, 1.0, src, ALU.mult, ALU.mult,
                        accum_out=b_stat[:, 1:2]))
                C(eg.tensor_scalar(b_stat[:, 2:3], b_stat[:, 0:1], INV_D,
                                   None, ALU.mult))
                C(eg.tensor_scalar(b_stat[:, 3:4], b_stat[:, 1:2], INV_D,
                                   None, ALU.mult))
                C(eg.tensor_tensor(b_stat[:, 4:5], b_stat[:, 2:3],
                                   b_stat[:, 2:3], ALU.mult))
                return eg.tensor_tensor(b_stat[:, 5:6], b_stat[:, 3:4],
                                        b_stat[:, 4:5], ALU.subtract)
            v1 = step("vector", list(waits), stats)
            a1 = step("scalar", [v1], lambda eg: eg.activation(
                b_stat[:, 6:7], b_stat[:, 5:6], AF.Sqrt, bias=b_eps))

            def norm(eg):
                C(eg.reciprocal(b_stat[:, 7:8], b_stat[:, 6:7]))
                return eg.tensor_scalar(dst, src, b_stat[:, 2:3], b_stat[:, 7:8],
                                        ALU.subtract, ALU.mult)
            return step("vector", [a1], norm)

        def phaseV(ph):
            b_x = [sbuf(f"pv_x{i}", [128, D], F32, ph) for i in range(2)]
            b_ln1 = [sbuf(f"pv_ln1{i}", [128, D], BF16, ph) for i in range(2)]
            b_sq = sbuf("pv_sq", [128, D], F32, ph)
            b_ln1T = [sbuf(f"pv_ln1T{c}", [128, 512], BF16, ph) for c in range(NC)]
            b_wv8 = sbuf("pv_wv8", [128, NC * D], BF16, ph)
            b_vstg = [sbuf(f"pv_vstg{i}", [128, H * 65], BF16, ph) for i in range(2)]
            b_vsacc = sbuf("pv_vsacc", [1, 2048], F32, ph)
            b_tcs = sbuf("pv_tcs", [NT, H * 65], F32, ph)
            b_S32 = sbuf("pv_S32", [NT, NT], F32, ph)
            b_vstmp = [sbuf(f"pv_vt{i}", [1, H * 65], F32, ph) for i in range(2)]
            b_vsrow = (sbuf("pv_vsrow", [1, H * 65], F32, ph)
                       if "vsum" in debug_taps else None)
            p_proj = [ph.enter_context(
                nc.psum_tensor(f"pv_pj{i}", [128, 512], F32))[:] for i in range(2)]
            p_tp2 = [ph.enter_context(
                nc.psum_tensor(f"pv_tp{i}", [128, 128], BF16))[:] for i in range(2)]
            p_vs = ph.enter_context(nc.psum_tensor("pv_vs", [1, 2048], F32))[:]

            def body(he):
                own, dma, idma, step = he.own, he.dma, he.idma, he.step
                C = he.C
                wv_ld = None
                for jc in range(NC):
                    wv_ld = dma("sync", "wv8", b_wv8[:, jc * D:(jc + 1) * D],
                                s_wvoT[0, jc])
                s32_ld = dma("sync", "cst", b_S32, c_S32)
                ln1_done, st_ln1, tp_last = {}, {}, {}
                tpev = {}
                tplist = []
                pj_ev = []
                vst_free = {}
                vsadd = {}
                tcs_st = {}
                vproj_last = [None]
                for g in range(NG):
                    for tb in range(4):
                        i = g * 4 + tb
                        bx, bl = b_x[i % 2], b_ln1[i % 2]
                        xv = dma("sync", "x%d" % (i % 2), bx,
                                 dx[i * 128:(i + 1) * 128, :],
                                 waits=[ln1_done.get(i - 2)])
                        lnw = [xv, st_ln1.get(i - 2), tp_last.get(i - 2)]
                        ln1_done[i] = ln_seq(he, bx, bl, b_sq, waits=lnw)
                        st_ln1[i] = dma("scalar", "stl%d" % (i % 2), s_ln1[i], bl,
                                        waits=[ln1_done[i]])
                        if t_ln1 is not None:
                            st_ln1[i] = dma("scalar", "stl%d" % (i % 2),
                                            t_ln1[i], bl)
                        for jc in range(NC):
                            sl = len(tplist) % 2
                            w = [ln1_done[i],
                                 tplist[-2] if len(tplist) >= 2 else None]
                            mv = step("tensor", w, lambda eg, jc=jc, bl=bl, sl=sl:
                                      eg.transpose(
                                          p_tp2[sl],
                                          bl[:, jc * 128:(jc + 1) * 128], b_identb))
                            tpev[(i, jc)] = step(
                                "vector", [mv, vproj_last[0]],
                                lambda eg, jc=jc, tb=tb, sl=sl: eg.tensor_copy(
                                    b_ln1T[jc][:, tb * 128:(tb + 1) * 128],
                                    p_tp2[sl]))
                            tplist.append(tpev[(i, jc)])
                            tp_last[i] = mv
                    for tb in range(4):
                        i = g * 4 + tb
                        bst = b_vstg[i % 2]
                        for hf in range(2):
                            k = (tb * 2 + hf) % 2
                            w = [wv_ld, vst_free.get(i - 2)]
                            w += [tpev[(i, jc)] for jc in range(NC)]
                            if len(pj_ev) >= 2:
                                w.append(pj_ev[-2])

                            def vproj(eg, tb=tb, hf=hf, k=k):
                                r = None
                                for jc in range(NC):
                                    r = eg.matmul(
                                        p_proj[k],
                                        b_ln1T[jc][:, tb * 128:(tb + 1) * 128],
                                        b_wv8[:, jc * D + hf * 512:
                                              jc * D + (hf + 1) * 512],
                                        start=(jc == 0), stop=(jc == NC - 1))
                                return r
                            mv = step("tensor", w, vproj)
                            vproj_last[0] = mv

                            def vevac(eg, bst=bst, hf=hf, k=k):
                                r = None
                                for hh in range(8):
                                    h = hf * 8 + hh
                                    if r is not None:
                                        C(r)
                                    r = eg.scalar_tensor_tensor(
                                        bst[:, h * 65: h * 65 + 64],
                                        p_proj[k][:, hh * 64:(hh + 1) * 64], 1.0,
                                        ROWS["bv"][:, h * 64:(h + 1) * 64],
                                        ALU.mult, ALU.add)
                                return r
                            ev = step("vector", [mv], vevac)
                            pj_ev.append(ev)
                        onesv = step("vector", [], lambda eg, bst=bst: eg.memset(
                            bst.rearrange("p (h k) -> p h k", k=65)[:, :, 64:65], 1.0))
                        stv = dma("scalar", "stv%d" % (i % 2), s_vext[i], bst,
                                  waits=[onesv])
                        if t_vext is not None:
                            stv = dma("scalar", "stv%d" % (i % 2), t_vext[i], bst)

                        def colsum(eg, bst=bst, i=i):
                            r = None
                            for h in range(H):
                                if r is not None:
                                    pass
                                r = eg.matmul(
                                    p_vs[0:1, h * 128: h * 128 + 65],
                                    b_onescolb, bst[:, h * 65:(h + 1) * 65],
                                    start=True, stop=True)
                            return r
                        mc = step("tensor", [onesv, vsadd.get(i - 1)], colsum)

                        def accum(eg, i=i):
                            va = b_vsacc.rearrange("o (h k) -> o h k", k=128)[:, :, 0:65]
                            pv = p_vs.rearrange("o (h k) -> o h k", k=128)[:, :, 0:65]
                            C(eg.tensor_copy(
                                b_vstmp[i % 2].rearrange("o (h k) -> o h k", k=65),
                                pv))
                            if i == 0:
                                C(eg.tensor_copy(va, pv))
                            else:
                                C(eg.tensor_tensor(va, va, pv, ALU.add))
                            return eg.nop()
                        vsadd[i] = step("vector",
                                        [mc, tcs_st.get(i - 2)], accum)
                        tcs_st[i] = dma("scalar", "tc%d" % (i % 2),
                                        s_tcs[i:i + 1], b_vstmp[i % 2],
                                        waits=[vsadd[i]])
                        vst_free[i] = step("vector", [stv, mc], lambda eg: None)
                if t_vsum is not None:
                    vs = step("vector", [(MM, MM.val)], lambda eg: eg.tensor_copy(
                        b_vsrow.rearrange("o (h k) -> o h k", k=65),
                        b_vsacc.rearrange("o (h k) -> o h k", k=128)[:, :, 0:65]))
                    dma("scalar", "tap", t_vsum, b_vsrow, waits=[vs])
                tcld = dma("sync", "cst", b_tcs, s_tcs,
                           waits=[tcs_st[NT - 1], tcs_st[NT - 2], s32_ld])
                prev_ev = None
                for ch in range(3):
                    c0 = ch * 512
                    cw = min(512, H * 65 - c0)
                    mm = step("tensor",
                              [s32_ld, tcld, prev_ev,
                               pj_ev[-1], pj_ev[-2]],
                              lambda eg, c0=c0, cw=cw: eg.matmul(
                                  p_proj[0][0:NT, 0:cw], b_S32,
                                  b_tcs[0:NT, c0:c0 + cw],
                                  start=True, stop=True))
                    prev_ev = step("vector", [mm],
                                   lambda eg, c0=c0, cw=cw: eg.tensor_copy(
                                       b_tcs[0:NT, c0:c0 + cw],
                                       p_proj[0][0:NT, 0:cw]))
                dma("sync", "cst", s_adj, b_tcs, waits=[prev_ev, tcld])
            return body

        if stop_after == '0':
            return nc, dbg
        run_phase(phaseV)

        wconv_units = []
        for e in range(E):
            for jc in range(NC):
                wconv_units.append((
                    s_w1s[e * 128:(e + 1) * 128, jc * FF:(jc + 1) * FF]
                    .rearrange("p (s f) -> p s f", f=2048),
                    dW1[e, jc * 128:(jc + 1) * 128, :]
                    .rearrange("p (s f) -> p s f", f=2048)))
        for e in range(E):
            for fq in range(4):
                wconv_units.append((
                    s_w2s[e * 128:(e + 1) * 128, fq * 8192:(fq + 1) * 8192]
                    .rearrange("p (c o) -> p c o", o=D),
                    dW2[e, fq * 1024:(fq + 1) * 1024, :]
                    .rearrange("(c p) o -> p c o", p=128)))

        def phaseA(ph):
            b_ln1g = [sbuf(f"pa_l1{i}", [128, D], BF16, ph) for i in range(2)]
            b_LX = [sbuf(f"pa_lx{c}", [128, 512], BF16, ph) for c in range(NC)]
            b_qT = [sbuf(f"pa_qT{c}", [128, 512], BF16, ph) for c in range(NC)]
            b_kT = [sbuf(f"pa_kT{c}", [128, KWIN * 128], BF16, ph) for c in range(NC)]
            b_vring = [sbuf(f"pa_vr{m}", [128, H * 65], BF16, ph) for m in range(KWIN)]
            b_wsl = [sbuf(f"pa_wsl{i}", [128, D], BF16, ph) for i in range(2)]
            b_wo8 = sbuf("pa_wo8", [128, NC * D], BF16, ph)
            b_wpre = [sbuf(f"pa_wp{i}", [128, 768], BF16, ph) for i in range(2)]
            b_aexp = [sbuf(f"pa_ae{i}", [128, 768], BF16, ph) for i in range(2)]
            b_ao = [sbuf(f"pa_ao{i}", [128, D], BF16, ph) for i in range(2)]
            b_T8 = [sbuf(f"pa_t8{c}", [128, 512], BF16, ph) for c in range(NC)]
            b_x = [sbuf(f"pa_x{i}", [128, D], F32, ph) for i in range(2)]
            b_o = sbuf("pa_o", [128, D], F32, ph)
            b_t0 = [sbuf(f"pa_t0{i}", [128, D], F32, ph) for i in range(2)]
            b_t1 = [sbuf(f"pa_t1{i}", [128, D], F32, ph) for i in range(2)]
            b_xn = sbuf("pa_xn", [128, D], F32, ph)
            b_xnb = [sbuf(f"pa_xnb{i}", [128, D], BF16, ph) for i in range(2)]
            b_gf = sbuf("pa_gf", [128, 128], F32, ph)
            b_fT = sbuf("pa_fT", [128, 128], F32, ph)
            b_rc = sbuf("pa_rc", [128, 8], F32, ph)
            b_rcu0 = sbuf("pa_rcu0", [128, 4], U32, ph)
            b_rcu1 = sbuf("pa_rcu1", [128, 4], U32, ph)
            b_sidx0 = sbuf("pa_sidx0", [128, 1], U32, ph)
            b_sidx1 = sbuf("pa_sidx1", [128, 1], U32, ph)
            p_QK = ph.enter_context(nc.psum_tensor("pa_QK", [128, 2048], F32))[:]
            p_AV = [ph.enter_context(
                nc.psum_tensor(f"pa_AV{i}", [128, 512], F32))[:] for i in range(2)]
            p_tp2 = [ph.enter_context(
                nc.psum_tensor(f"pa_tp{i}", [128, 128], BF16))[:] for i in range(2)]
            p_R = p_QK
            p_tpf = p_QK[:, 640:768]
            p_cum = p_QK[:, 776:784]
            b_adjq = [sbuf(f"pa_adj{i}", [1, 512], F32, ph) for i in range(2)]

            def body(he):
                own, dma, idma, step = he.own, he.dma, he.idma, he.step
                C = he.C
                eg = he.eg

                wo_ld = None
                for jc in range(NC):
                    wo_ld = dma("sync", "wo", b_wo8[:, jc * D:(jc + 1) * D],
                                s_wvoT[1, jc])
                ac_exp, ve_mask, av_m, qk_m = {}, {}, {}, {}
                ve_div, rk_ev, adq = {}, {}, {}
                vload_t = {}
                adjini = step("gpsimd", [], lambda eg: (
                    C(eg.memset(b_adjq[0], 0.0)),
                    eg.memset(b_adjq[1], 0.0))[-1])
                aotev, tp_ev = {}, {}
                ln1g_tp, ta_st, sl2st, sxn_st, xn_tp_last = {}, {}, {}, {}, {}
                tplist = []

                def transpose_pipe(src_slice, dst_slice, waits):
                    sl = len(tplist) % 2
                    w = list(waits) + [tplist[-2] if len(tplist) >= 2 else None]
                    mv = step("tensor", w, lambda eg: eg.transpose(
                        p_tp2[sl], src_slice, b_identb))
                    ev = step("vector", [mv], lambda eg: eg.tensor_copy(
                        dst_slice, p_tp2[sl]))
                    tplist.append(ev)
                    return mv, ev
                chain_end = {}
                gbar = None
                for g in range(NG):
                    if gbar is not None and not (he.own("sync")
                                                  or he.own("gpsimd")):
                        he.do_waits(gbar)
                    for u in range(g * 12, min((g + 1) * 12, len(wconv_units))):
                        dma("gpsimd", "wc", wconv_units[u][0], wconv_units[u][1])
                    lnld, vload = {}, {}
                    for tb in range(4):
                        i = g * 4 + tb
                        lnld[tb] = dma("sync", "lg%d" % (tb % 2), b_ln1g[tb % 2],
                                       s_ln1[i],
                                       waits=[ln1g_tp.get(i - 2)])
                        for jc in range(NC):
                            mv, ev = transpose_pipe(
                                b_ln1g[tb % 2][:, jc * 128:(jc + 1) * 128],
                                b_LX[jc][:, tb * 128:(tb + 1) * 128],
                                [lnld[tb]])
                            tp_ev[(tb, jc)] = ev
                            ln1g_tp[i] = mv
                        vload[tb] = dma(
                            "sync", "vr%d" % (i % 2), b_vring[i % KWIN],
                            s_vext[i],
                            waits=[vload_t.get(i - 2),
                                   av_m.get((i - 4) * H + H - 1)])
                        vload_t[i] = vload[tb]
                    vload_all = [vload[2], vload[3]]
                    lxr = max([tp_ev[(tb, jc)] for tb in range(4)
                               for jc in range(NC)], key=lambda t: t[1])
                    pj_ev = []
                    pj_mv = []
                    for wsel, bias in ((0, b_bq), (1, b_bk)):
                        for oc in range(NC):
                            wv = dma("sync", "ws%d" % (oc % 2), b_wsl[oc % 2],
                                     s_wqkT[wsel, oc],
                                     waits=[pj_mv[-2] if len(pj_mv) >= 2 else None])
                            w = [wv, lxr]
                            if len(pj_ev) >= 2:
                                w.append(pj_ev[-2])

                            def proj(eg, oc=oc):
                                r = None
                                for jc in range(NC):
                                    r = eg.matmul(
                                        p_QK[:, (oc % 2) * 1024:
                                             (oc % 2) * 1024 + 512],
                                        b_wsl[oc % 2][:, jc * 128:(jc + 1) * 128],
                                        b_LX[jc], start=(jc == 0), stop=(jc == NC - 1))
                                return r
                            mv = step("tensor", w, proj)
                            pj_mv.append(mv)
                            if wsel == 0:
                                ev = step("vector", [mv],
                                          lambda eg, oc=oc: eg.tensor_scalar(
                                              b_qT[oc],
                                              p_QK[:, (oc % 2) * 1024:
                                                   (oc % 2) * 1024 + 512],
                                              b_bq[:, oc:oc + 1], None, ALU.add))
                            else:
                                base = (g * 4) % KWIN
                                ev = step("vector", [mv],
                                          lambda eg, oc=oc, base=base:
                                          eg.tensor_scalar(
                                              b_kT[oc][:, base * 128:
                                                       base * 128 + 512],
                                              p_QK[:, (oc % 2) * 1024:
                                                   (oc % 2) * 1024 + 512],
                                              b_bk[:, oc:oc + 1], None, ALU.add))
                            pj_ev.append(ev)
                    pjr = pj_ev[-1]
                    for tb in range(4):
                        i = g * 4 + tb
                        valid = [c for c in CLIST if i >= c]
                        nv = len(valid)
                        def adjdma(q, i=i):
                            pi, pq = (i, q - 2) if q >= 2 else (i - 1, q + 2)
                            adq[(i, q)] = dma(
                                "sync", "aj%d" % (q % 2),
                                b_adjq[q % 2].rearrange(
                                    "o (h k) -> o h k", k=128)[:, :, 0:65],
                                s_adj[i:i + 1].rearrange(
                                    "o (h k) -> o h k", k=65)[:, q * 4:(q + 1) * 4, :],
                                waits=[rk_ev.get((pi, pq)), adjini])
                        adjdma(0)
                        adjdma(1)

                        def rank1(q, i=i):
                            pi, pq = (i, q - 2) if q >= 2 else (i - 1, q + 2)
                            puhs = pi * H + pq * 4 + 3
                            w = [adq[(i, q)], ve_div.get(puhs)]
                            rk_ev[(i, q)] = step(
                                "tensor", w,
                                lambda eg, q=q, i=i: eg.matmul(
                                    p_AV[q % 2], b_ones1f, b_adjq[q % 2],
                                    start=True, stop=False))

                        def avgroup(h, i=i, valid=valid):
                            hs = i * H + h
                            slot = (h % 4) * 128
                            pav = p_AV[(h // 4) % 2]
                            w = vload_all + [ac_exp[hs], rk_ev[(i, h // 4)]]

                            def fn(eg, h=h, hs=hs, slot=slot, pav=pav,
                                   valid=valid, i=i):
                                r = None
                                for cp, c in enumerate(valid):
                                    r = eg.matmul(
                                        pav[:, slot:slot + 65],
                                        b_aexp[hs % 2][:, cp * 128:(cp + 1) * 128],
                                        b_vring[(i - c) % KWIN][:, h * 65:(h + 1) * 65],
                                        start=False,
                                        stop=(cp == len(valid) - 1
                                              and h % 4 == 3))
                                return r
                            av_m[hs] = step("tensor", w, fn)

                        def divides(q, i=i, tb=tb):
                            lasthq = i * H + q * 4 + 3
                            for hh in range(4):
                                h = q * 4 + hh
                                hs = i * H + h
                                slot = (h % 4) * 128
                                pav = p_AV[q % 2]
                                w = [av_m[lasthq],
                                     aotev.get((i - 2, NC - 1)),
                                     ta_st.get(i - 2)]

                                def div(eg, pav=pav, slot=slot, tb=tb, h=h):
                                    C(eg.reciprocal(b_stat[:, 8:9],
                                                    pav[:, slot + 64:slot + 65]))
                                    return eg.tensor_scalar(
                                        b_ao[tb % 2][:, h * 64:(h + 1) * 64],
                                        pav[:, slot:slot + 64],
                                        b_stat[:, 8:9], None, ALU.mult)
                                ve_div[hs] = step("vector", w, div)

                        for h in range(H):
                            hs = i * H + h
                            hc, hb = h // 2, h % 2
                            if h % 4 == 0:
                                rank1(h // 4)
                                if h // 4 < 2:
                                    adjdma(h // 4 + 2)
                            w = [ve_mask.get(hs - 2), pjr]

                            def qk(eg, i=i, tb=tb, hc=hc, hb=hb, valid=valid,
                                   h=h):
                                r = None
                                qb = (h % 2) * 1024
                                for cp, c in enumerate(valid):
                                    sl = (i - c) % KWIN
                                    r = eg.matmul(
                                        p_QK[:, qb + cp * 128:qb + (cp + 1) * 128],
                                        b_kT[hc][hb * 64:(hb + 1) * 64,
                                                 sl * 128:(sl + 1) * 128],
                                        b_qT[hc][hb * 64:(hb + 1) * 64,
                                                 tb * 128:(tb + 1) * 128],
                                        start=True, stop=True)
                                return r
                            qk_m[hs] = step("tensor", w, qk)
                            if h > 0:
                                avgroup(h - 1)
                            w = [qk_m[hs], ac_exp.get(hs - 2)]
                            ve_mask[hs] = step(
                                "vector", w,
                                lambda eg, hs=hs, nv=nv, h=h:
                                eg.scalar_tensor_tensor(
                                    b_wpre[hs % 2][:, 0:nv * 128],
                                    p_QK[:, (h % 2) * 1024:
                                         (h % 2) * 1024 + nv * 128], 0.125,
                                    b_masks[:, 0:nv * 128], ALU.mult, ALU.mult))
                            w = [ve_mask[hs], av_m.get(hs - 2)]
                            ac_exp[hs] = step(
                                "scalar", w,
                                lambda eg, hs=hs, nv=nv: eg.activation(
                                    b_aexp[hs % 2][:, 0:nv * 128],
                                    b_wpre[hs % 2][:, 0:nv * 128], AF.Exp))
                            if h % 4 == 0 and h > 0:
                                divides(h // 4 - 1)
                        avgroup(H - 1)
                        divides(3)
                        if t_ao is not None:
                            ta_st[i] = dma("scalar", "ta%d" % (tb % 2), t_ao[i],
                                           b_ao[tb % 2],
                                           waits=[ve_div[i * H + H - 1]])
                        for jc in range(NC):
                            mv, ev = transpose_pipe(
                                b_ao[tb % 2][:, jc * 128:(jc + 1) * 128],
                                b_T8[jc][:, tb * 128:(tb + 1) * 128],
                                [ve_div[i * H + H - 1]])
                            aotev[(i, jc)] = mv
                            tp_ev[("ao", i, jc)] = ev
                    last_div = ve_div[(g * 4 + 3) * H + H - 1]
                    oproj_ev = []
                    x2st = {}
                    for tb in range(4):
                        i = g * 4 + tb
                        for hf in range(2):
                            w = [last_div, wo_ld]
                            w += [tp_ev[("ao", i, jc)] for jc in range(NC)]
                            if len(oproj_ev) >= 2:
                                w.append(oproj_ev[-2])

                            kk = (tb * 2 + hf) % 2

                            def oproj(eg, tb=tb, hf=hf, kk=kk):
                                r = None
                                for jc in range(NC):
                                    r = eg.matmul(
                                        p_AV[kk][:, 0:512],
                                        b_T8[jc][:, tb * 128:(tb + 1) * 128],
                                        b_wo8[:, jc * D + hf * 512:
                                              jc * D + (hf + 1) * 512],
                                        start=(jc == 0), stop=(jc == NC - 1))
                                return r
                            mv = step("tensor", w, oproj)
                            ev = step("vector", [mv],
                                      lambda eg, tb=tb, hf=hf, kk=kk:
                                      eg.scalar_tensor_tensor(
                                          b_o[:, hf * 512:(hf + 1) * 512],
                                          p_AV[kk][:, 0:512], 1.0,
                                          ROWS["bo"][:, hf * 512:(hf + 1) * 512],
                                          ALU.mult, ALU.add,
                                          accum_out=b_stat[:, 8 + hf:9 + hf]))
                            oproj_ev.append(ev)
                        xv = dma("sync", "x%d" % (tb % 2), b_x[tb % 2],
                                 dx[i * 128:(i + 1) * 128, :],
                                 waits=[chain_end.get(i - 2)])
                        sq1 = step("vector", [], lambda eg: (
                            C(eg.tensor_tensor(b_stat[:, 0:1], b_stat[:, 8:9],
                                               b_stat[:, 9:10], ALU.add)),
                            eg.scalar_tensor_tensor(
                                b_xn, b_o, 1.0, b_o, ALU.mult, ALU.mult,
                                accum_out=b_stat[:, 1:2]))[-1])
                        y1 = ln_seq(he, b_o, b_xn, None, waits=[sq1],
                                    sums_precomputed=True)
                        x2v = step("vector", [xv, x2st.get(tb - 2)],
                                   lambda eg, tb=tb: eg.scalar_tensor_tensor(
                                       b_t0[tb % 2], b_xn, 1.0, b_x[tb % 2],
                                       ALU.mult, ALU.add,
                                       accum_out=b_stat[:, 0:1]))
                        x2st[tb] = dma("scalar", "sx%d" % (tb % 2), s_x2[i],
                                       b_t0[tb % 2], waits=[x2v])
                        if t_x2 is not None:
                            x2st[tb] = dma("scalar", "sx%d" % (tb % 2), t_x2[i],
                                           b_t0[tb % 2])
                        sq2 = step("vector", [], lambda eg, tb=tb:
                                   eg.scalar_tensor_tensor(
                                       b_xn, b_t0[tb % 2], 1.0, b_t0[tb % 2],
                                       ALU.mult, ALU.mult,
                                       accum_out=b_stat[:, 1:2]))
                        ln2v = ln_seq(he, b_t0[tb % 2], b_t1[tb % 2], None,
                                      waits=[sq2, sl2st.get(tb - 2)],
                                      sums_precomputed=True)
                        sl2st[tb] = dma("sync", "sl2%d" % (tb % 2), s_ln2[i],
                                        b_t1[tb % 2], waits=[ln2v])
                        if t_ln2 is not None:
                            sl2st[tb] = dma("sync", "sl2%d" % (tb % 2), t_ln2[i],
                                            b_t1[tb % 2])
                        xnv = step("vector", [sxn_st.get(tb - 2),
                                                xn_tp_last.get(tb - 2)],
                                   lambda eg, tb=tb: (
                            C(eg.tensor_tensor(b_xn, b_t1[tb % 2], ROWS["gam"],
                                               ALU.mult)),
                            eg.scalar_tensor_tensor(
                                b_xnb[tb % 2], b_xn, 1.0, ROWS["bet"],
                                ALU.mult, ALU.add))[-1])
                        sxn_st[tb] = dma("scalar", "sxn%d" % (tb % 2),
                                         s_xn[i * 128:(i + 1) * 128, :],
                                         b_xnb[tb % 2], waits=[xnv])
                        for jc in range(NC):
                            mv, ev = transpose_pipe(
                                b_xnb[tb % 2][:, jc * 128:(jc + 1) * 128],
                                b_LX[jc][:, tb * 128:(tb + 1) * 128],
                                [xnv])
                            xn_tp_last[tb] = mv
                            tp_ev[("xn", i, jc)] = ev
                        w = [chain_end.get(i - 1), last_div]
                        w += [tp_ev[("xn", i, jc)] for jc in range(NC)]

                        def router(eg, tb=tb):
                            r = None
                            for jc in range(NC):
                                r = eg.matmul(
                                    p_R[:, 0:8],
                                    b_LX[jc][:, tb * 128:(tb + 1) * 128],
                                    b_wr[:, jc * 8:(jc + 1) * 8],
                                    start=(jc == 0), stop=(jc == NC - 1))
                            return r
                        rt = step("tensor", w, router)
                        lg = step("vector", [rt], lambda eg: (
                            eg.tensor_tensor(b_r8, p_R[:, 0:8], b_brrep, ALU.add))
                        )
                        gv = step("vector", [], lambda eg, i=i: (
                            C(eg.reduce_max(b_stat[:, 10:11], b_r8, axis=AX.X)),
                            eg.tensor_scalar(b_stat[:, 11:12], b_stat[:, 10:11],
                                             -1.0, None, ALU.mult))[-1])
                        ex = step("scalar", [gv], lambda eg: eg.activation(
                            b_r8b, b_r8, AF.Exp, bias=b_stat[:, 11:12]))
                        def gates(eg, i=i):
                            C(eg.reduce_sum(b_stat[:, 10:11], b_r8b, axis=AX.X))
                            C(eg.reciprocal(b_stat[:, 11:12], b_stat[:, 10:11]))
                            C(eg.tensor_scalar(b_r8, b_r8b, b_stat[:, 11:12],
                                               None, ALU.mult))
                            C(eg.reduce_max(b_stat[:, 10:11], b_r8, axis=AX.X))
                            C(eg.tensor_scalar(b_r8b, b_r8, b_stat[:, 10:11],
                                               None, ALU.is_lt))
                            C(eg.tensor_tensor(b_r8b, b_r8, b_r8b, ALU.mult))
                            C(eg.reduce_max(b_stat[:, 12:13], b_r8b, axis=AX.X))
                            C(eg.tensor_scalar(b_r8c, b_r8, b_stat[:, 12:13],
                                               None, ALU.is_ge))
                            C(eg.tensor_tensor(b_g[:, i * 8:(i + 1) * 8], b_r8,
                                               b_r8c, ALU.mult))
                            return eg.reduce_sum(b_G[:, i:i + 1],
                                                 b_g[:, i * 8:(i + 1) * 8],
                                                 axis=AX.X)
                        gv2 = step("vector", [ex], gates)
                        pf = step("tensor", [gv2], lambda eg: eg.matmul(
                            p_R[:, 8:16], b_LT, b_r8c, start=True, stop=True))
                        cn = step("tensor", [], lambda eg: eg.matmul(
                            p_R[0:1, 512:520], b_onescolf, b_r8c,
                            start=True, stop=True))
                        cp = step("vector", [gv2], lambda eg: eg.tensor_copy(
                            b_gf[:, 0:8], b_r8c))
                        ft = step("tensor", [cp], lambda eg: eg.transpose(
                            p_tpf, b_gf, b_identf))
                        fe = step("vector", [ft], lambda eg: eg.tensor_copy(
                            b_fT[0:8, :], p_tpf[0:8, :]))
                        cu = step("tensor", [fe], lambda eg: eg.matmul(
                            p_cum, b_fT[0:8, :], b_U8,
                            start=True, stop=True))

                        def a4(eg, i=i):
                            P = p_R[:, 8:16]
                            A = C
                            A(eg.tensor_copy(b_cnt[0:1, i * 8:(i + 1) * 8],
                                             p_R[0:1, 512:520]))
                            cum = p_cum
                            A(eg.tensor_scalar(b_r8b, cum, 0.0, None, ALU.is_equal))
                            A(eg.tensor_tensor(b_r8b, b_r8b, b_r8c, ALU.mult))
                            A(eg.tensor_tensor(b_gf[:, 8:16], b_r8b, b_evec,
                                               ALU.mult))
                            A(eg.reduce_sum(b_stat[:, 10:11], b_gf[:, 8:16],
                                            axis=AX.X))
                            A(eg.tensor_tensor(b_gf[:, 8:16], b_r8b, P, ALU.mult))
                            A(eg.reduce_sum(b_stat[:, 11:12], b_gf[:, 8:16],
                                            axis=AX.X))
                            A(eg.tensor_tensor(b_gf[:, 8:16], b_r8b, b_r8,
                                               ALU.mult))
                            A(eg.reduce_sum(b_stat[:, 12:13], b_gf[:, 8:16],
                                            axis=AX.X))
                            A(eg.tensor_scalar(b_r8b, cum, 1.0, None, ALU.is_equal))
                            A(eg.tensor_tensor(b_r8b, b_r8b, b_r8c, ALU.mult))
                            A(eg.tensor_tensor(b_gf[:, 8:16], b_r8b, b_evec,
                                               ALU.mult))
                            A(eg.reduce_sum(b_stat[:, 13:14], b_gf[:, 8:16],
                                            axis=AX.X))
                            A(eg.tensor_tensor(b_gf[:, 8:16], b_r8b, P, ALU.mult))
                            A(eg.reduce_sum(b_stat[:, 14:15], b_gf[:, 8:16],
                                            axis=AX.X))
                            A(eg.tensor_tensor(b_gf[:, 8:16], b_r8b, b_r8,
                                               ALU.mult))
                            A(eg.reduce_sum(b_stat[:, 15:16], b_gf[:, 8:16],
                                            axis=AX.X))
                            A(eg.tensor_scalar(b_rc[:, 0:1], b_iota, float(i * 128),
                                               None, ALU.add))
                            A(eg.tensor_copy(b_rc[:, 1:2], b_rc[:, 0:1]))
                            A(eg.tensor_scalar(b_rc[:, 2:3], b_stat[:, 12:13],
                                               GSCALE, None, ALU.mult))
                            A(eg.tensor_copy(b_rc[:, 4:5], b_rc[:, 0:1]))
                            A(eg.tensor_scalar(b_rc[:, 5:6], b_rc[:, 0:1], float(T),
                                               None, ALU.add))
                            A(eg.tensor_scalar(b_rc[:, 6:7], b_stat[:, 15:16],
                                               GSCALE, None, ALU.mult))
                            A(eg.memset(b_rc[:, 3:4], 0.0))
                            A(eg.memset(b_rc[:, 7:8], 0.0))
                            A(eg.tensor_scalar(b_gf[:, 16:17], b_stat[:, 10:11],
                                               128.0, float(i * 1024),
                                               ALU.mult, ALU.add))
                            A(eg.tensor_tensor(b_gf[:, 16:17], b_gf[:, 16:17],
                                               b_stat[:, 11:12], ALU.add))
                            A(eg.tensor_scalar(b_gf[:, 17:18], b_stat[:, 13:14],
                                               128.0, float(i * 1024),
                                               ALU.mult, ALU.add))
                            A(eg.tensor_tensor(b_gf[:, 17:18], b_gf[:, 17:18],
                                               b_stat[:, 14:15], ALU.add))
                            A(eg.tensor_copy(b_rcu0, b_rc[:, 0:4]))
                            A(eg.tensor_copy(b_rcu1, b_rc[:, 4:8]))
                            A(eg.tensor_copy(b_sidx0, b_gf[:, 16:17]))
                            return eg.tensor_copy(b_sidx1, b_gf[:, 17:18])
                        cev = step("vector", [cu, chain_end.get(("sc", i - 1))],
                                   a4)
                        chain_end[i] = cev
                        sc0 = idma("sc", s_recs, bass.IndirectOffsetOnAxis(
                            ap=b_sidx0, axis=0), b_rcu0, None,
                            waits=[cev], bounds_check=NRECS - 1,
                            oob_is_err=False)
                        sc1 = idma("sc", s_recs, bass.IndirectOffsetOnAxis(
                            ap=b_sidx1, axis=0), b_rcu1, None,
                            bounds_check=NRECS - 1, oob_is_err=False)
                        chain_end[("sc", i)] = step("gpsimd", [sc1],
                                                    lambda eg: None)
                    gbar = [(s, s.val) for s in ALLSEMS]
                if t_g is not None:
                    dma("scalar", "tap", t_g, b_g, waits=gbar)
                    dma("scalar", "tap", t_G, b_G)
                    dma("scalar", "tap", t_cnt, b_cnt)
            return body

        if stop_after == 'V':
            return nc, dbg
        run_phase(phaseA)
        mid.close()

        b_rbrep = sbuf("b_rbrep", [128, NRUN], F32)
        b_b8rep = sbuf("b_b8rep", [128, E], F32)
        b_cc2 = sbuf("b_cc2", [E + 1, D], F32)

        def phaseR(ph):
            b_ta = sbuf("pr_ta", [1, NT * E], F32, ph)
            b_tb = sbuf("pr_tb", [1, NT * E], F32, ph)
            b_ne = sbuf("pr_ne", [1, E], F32, ph)
            b_e8a = sbuf("pr_e8a", [1, E], F32, ph)
            b_e8b = sbuf("pr_e8b", [1, E], F32, ph)
            b_eu = sbuf("pr_eu", [1, E], U32, ph)
            b_base = sbuf("pr_base", [1, E], F32, ph)
            b_trb = sbuf("pr_trb", [1, NRUN], F32, ph)
            b_trs = sbuf("pr_trs", [1, NRUN], F32, ph)
            b_tdl = sbuf("pr_tdl", [1, NRUN], F32, ph)
            b_trash = sbuf("pr_trash", [128, 4], U32, ph)
            p_rep = ph.enter_context(nc.psum_tensor("pr_rep", [128, 512], F32))[:]

            def body(he):
                own, dma, idma, step = he.own, he.dma, he.idma, he.step
                C = he.C
                cv = lambda a: a.rearrange("o (i e) -> o i e", e=E)

                def build(eg):
                    A = C
                    A(eg.reduce_sum(b_ne.rearrange("o (e u) -> o e u", u=1),
                                  b_cnt.rearrange("o (i e) -> o e i", e=E),
                                  axis=AX.X))
                    A(eg.tensor_scalar(b_e8a, b_ne, 255.0, 1.0 / 256.0,
                                     ALU.add, ALU.mult))
                    A(eg.tensor_copy(b_eu, b_e8a))
                    A(eg.tensor_copy(b_e8a, b_eu))
                    A(eg.tensor_scalar(b_e8a, b_e8a, 256.0, None, ALU.mult))
                    A(eg.tensor_copy(b_e8b[0:1, 0:1], b_e8a[0:1, 0:1]))
                    A(eg.tensor_tensor(b_e8b[0:1, 1:8], b_e8a[0:1, 1:8],
                                     b_e8a[0:1, 0:7], ALU.add))
                    A(eg.tensor_copy(b_e8a[0:1, 0:2], b_e8b[0:1, 0:2]))
                    A(eg.tensor_tensor(b_e8a[0:1, 2:8], b_e8b[0:1, 2:8],
                                     b_e8b[0:1, 0:6], ALU.add))
                    A(eg.tensor_copy(b_e8b[0:1, 0:4], b_e8a[0:1, 0:4]))
                    A(eg.tensor_tensor(b_e8b[0:1, 4:8], b_e8a[0:1, 4:8],
                                     b_e8a[0:1, 0:4], ALU.add))
                    A(eg.memset(b_base[0:1, 0:1], 0.0))
                    A(eg.tensor_copy(b_base[0:1, 1:8], b_e8b[0:1, 0:7]))
                    A(eg.tensor_copy(b_ta, b_cnt))
                    src, dst = b_ta, b_tb
                    for sh in (1, 2, 4, 8, 16):
                        A(eg.tensor_copy(cv(dst)[:, 0:sh, :], cv(src)[:, 0:sh, :]))
                        A(eg.tensor_tensor(cv(dst)[:, sh:NT, :], cv(src)[:, sh:NT, :],
                                         cv(src)[:, 0:NT - sh, :], ALU.add))
                        src, dst = dst, src
                    A(eg.memset(cv(dst)[:, 0:1, :], 0.0))
                    A(eg.tensor_copy(cv(dst)[:, 1:NT, :], cv(src)[:, 0:NT - 1, :]))
                    for i in range(NT):
                        A(eg.tensor_tensor(cv(dst)[:, i, :], cv(dst)[:, i, :],
                                         b_base, ALU.add))
                    for e in range(E):
                        A(eg.tensor_copy(
                            b_trb[0:1, e * (NT + 1):e * (NT + 1) + NT],
                            dst.rearrange("o (i e) -> o e i", e=E)[0:1, e, :]))
                    A(eg.tensor_tensor(b_e8a, b_base, b_ne, ALU.add))
                    A(eg.tensor_copy(
                        b_trb.rearrange("o (e k) -> o e k", k=NT + 1)[0:1, :, NT:NT + 1],
                        b_e8a.rearrange("o (e u) -> o e u", u=1)))
                    return eg.tensor_tensor(b_tdl, b_trs, b_trb, ALU.subtract)
                v0 = dma("sync", "cst2", b_trs, c_runstart)
                bv = step("vector", [v0], build)
                st1 = dma("sync", "rt", s_runtab[:, 0:1].rearrange("r o -> o r"),
                          b_trb, waits=[bv], allow_slow_non_contiguous=True)
                st2 = dma("sync", "rt", s_runtab[:, 1:2].rearrange("r o -> o r"),
                          b_tdl, allow_slow_non_contiguous=True)
                if t_runtab is not None:
                    dma("sync", "rt", t_runtab[:, 0:1].rearrange("r o -> o r"),
                        b_trb, allow_slow_non_contiguous=True)
                    dma("sync", "rt", t_runtab[:, 1:2].rearrange("r o -> o r"),
                        b_tdl, allow_slow_non_contiguous=True)
                for hf in range((NRUN + 511) // 512):
                    lo, hi = hf * 512, min(NRUN, (hf + 1) * 512)
                    mv = step("tensor", [bv, (VE, VE.val)], lambda eg, lo=lo, hi=hi:
                              eg.matmul(p_rep[:, 0:hi - lo], b_ones1f,
                                        b_trb[0:1, lo:hi], start=True, stop=True))
                    step("vector", [mv], lambda eg, lo=lo, hi=hi:
                         eg.tensor_copy(b_rbrep[:, lo:hi], p_rep[:, 0:hi - lo]))
                mv = step("tensor", [(VE, VE.val)], lambda eg: eg.matmul(
                    p_rep[:, 0:E], b_ones1f, b_base, start=True, stop=True))
                step("vector", [mv], lambda eg: eg.tensor_copy(
                    b_b8rep, p_rep[:, 0:E]))
                tv = step("vector", [], lambda eg: (
                    C(eg.memset(b_trash, 0)),
                    eg.memset(b_trash[:, 1:2], OOBD))[-1])
                for k in range(16):
                    off = NT * E * 128 + k * 128
                    dma("sync", "rt", s_recs[off:off + 128, :], b_trash,
                        waits=[tv])
            return body

        if stop_after == 'A':
            return nc, dbg
        run_phase(phaseR)

        def phaseF(ph):
            b_w1buf = sbuf("pf_w1", [128, NC * FF], BF16, ph)
            b_w2buf = sbuf("pf_w2", [128, NF * D], BF16, ph)
            b_h = sbuf("pf_h", [128, NF * 256], BF16, ph)
            b_xnPT = [sbuf(f"pf_xnpt{i}", [128, NC * 256], BF16, ph) for i in range(2)]
            b_xnP = [sbuf(f"pf_xnp{i}", [128, D], BF16, ph) for i in range(4)]
            b_slot = [sbuf(f"pf_slot{i}", [128, 1], F32, ph) for i in range(4)]
            b_ge = sbuf("pf_ge", [128, NRUN], F32, ph)
            b_ridx = [sbuf(f"pf_ridx{i}", [128, 1], U32, ph) for i in range(4)]
            b_rp = [sbuf(f"pf_rp{i}", [128, 2], F32, ph) for i in range(4)]
            b_rec = [sbuf(f"pf_rec{i}", [128, 4], U32, ph) for i in range(4)]
            b_gp = [sbuf(f"pf_gp{i}", [128, 1], F32, ph) for i in range(4)]
            b_widx = [sbuf(f"pf_widx{i}", [128, 1], U32, ph) for i in range(2)]
            b_eprev = sbuf("pf_eprev", [128, 1], F32, ph)
            b_ecur = sbuf("pf_ecur", [128, 1], F32, ph)
            b_big = sbuf("pf_big", [128, 1], F32, ph)
            b_tmp = sbuf("pf_tmp", [128, 2], F32, ph)
            b_b1c = [sbuf(f"pf_b1c{i}", [128, NF], F32, ph) for i in range(2)]
            b_y = [sbuf(f"pf_y{i}", [128, D], F32, ph) for i in range(2)]
            b_rb1 = sbuf("pf_rb1", [128, NF], BF16, ph)
            b_b2r = sbuf("pf_b2r", [1, D], F32, ph)
            b_ccstg = sbuf("pf_ccstg", [1, D], F32, ph)
            b_cestg = sbuf("pf_cestg", [1, 512], F32, ph)
            b_ctacc = sbuf("pf_ctacc", [1, D], F32, ph)
            p_h = [ph.enter_context(nc.psum_tensor(f"pf_h{i}", [128, 256], F32))[:]
                   for i in range(2)]
            p_o = [ph.enter_context(nc.psum_tensor(f"pf_o{i}", [128, 512], F32))[:]
                   for i in range(2)]
            p_tp2 = [ph.enter_context(
                nc.psum_tensor(f"pf_tp{i}", [128, 128], BF16))[:] for i in range(2)]
            p_ce = ph.enter_context(nc.psum_tensor("pf_ce", [1, 512], F32))[:]

            def body(he):
                own, dma, idma, step = he.own, he.dma, he.idma, he.step
                C = he.C
                eg = he.eg
                iv = step("vector", [], lambda eg: (
                    C(eg.memset(b_eprev, -1.0)),
                    C(eg.memset(b_big, 1.0e7)),
                    eg.memset(b_ctacc, 0.0))[-1])
                chain_v, rp_g, rec_g, xnp_g = {}, {}, {}, {}
                w_g, tp_m, tp_v = {}, {}, {}
                tplist = []
                mm1_m, rl_m, mm2_m, yev, sct = {}, {}, {}, {}, {}
                b1_g = {}

                def chain1(tau):
                    def f(eg, tau=tau):
                        C(eg.tensor_scalar(b_slot[tau % 4], b_iota,
                                           float(tau * 128), None, ALU.add))
                        C(eg.tensor_scalar(b_ge, b_rbrep, b_slot[tau % 4], None,
                                           ALU.is_le))
                        C(eg.reduce_sum(b_stat[:, 0:1], b_ge, axis=AX.X))
                        C(eg.tensor_scalar(b_stat[:, 1:2], b_stat[:, 0:1], -1.0,
                                           None, ALU.add))
                        return eg.tensor_copy(b_ridx[tau % 4], b_stat[:, 1:2])
                    chain_v[("s1", tau)] = step(
                        "vector", [rec_g.get(tau - 4)], f)
                    rp_g[("rp", tau)] = idma(
                        "rp", b_rp[tau % 4], None, s_runtab,
                        bass.IndirectOffsetOnAxis(ap=b_ridx[tau % 4], axis=0),
                        waits=[chain_v[("s1", tau)]],
                        bounds_check=NRUN - 1, oob_is_err=False)

                    def f2(eg, tau=tau):
                        C(eg.tensor_tensor(b_stat[:, 2:3], b_slot[tau % 4],
                                           b_rp[tau % 4][:, 1:2], ALU.add))
                        return eg.tensor_copy(b_ridx[tau % 4], b_stat[:, 2:3])
                    chain_v[("s3", tau)] = step(
                        "vector", [rp_g[("rp", tau)]], f2)
                    rec_g[tau] = idma(
                        "rc", b_rec[tau % 4], None, s_recs,
                        bass.IndirectOffsetOnAxis(ap=b_ridx[tau % 4], axis=0),
                        waits=[chain_v[("s3", tau)], sct.get(tau - 4)],
                        bounds_check=NRECS - 1, oob_is_err=False)
                    xnp_g[tau] = idma(
                        "xp%d" % (tau % 4), b_xnP[tau % 4], None, s_xn,
                        bass.IndirectOffsetOnAxis(ap=b_rec[tau % 4][:, 0:1], axis=0),
                        waits=[rec_g[tau], tp_m.get((tau - 4, NC - 1))],
                        bounds_check=T - 1, oob_is_err=False)

                    def f3(eg, tau=tau):
                        C(eg.tensor_copy(b_stat[:, 3:4], b_rec[tau % 4][:, 2:3]))
                        return eg.tensor_scalar(b_gp[tau % 4], b_stat[:, 3:4],
                                                1.0 / GSCALE, None, ALU.mult)
                    chain_v[("g", tau)] = step(
                        "vector", [rec_g[tau]], f3)

                def wfetch(pr):
                    tau = 2 * pr
                    def f(eg, tau=tau, pr=pr):
                        C(eg.tensor_scalar(b_ge[:, 0:E], b_b8rep,
                                           b_slot[tau % 4], None, ALU.is_le))
                        C(eg.reduce_sum(b_stat[:, 4:5], b_ge[:, 0:E], axis=AX.X))
                        C(eg.tensor_scalar(b_ecur, b_stat[:, 4:5], -1.0, None,
                                           ALU.add))
                        C(eg.tensor_tensor(b_stat[:, 5:6], b_ecur, b_eprev,
                                           ALU.not_equal))
                        C(eg.tensor_scalar(b_tmp[:, 0:1], b_ecur, 128.0, None,
                                           ALU.mult))
                        C(eg.tensor_tensor(b_tmp[:, 0:1], b_tmp[:, 0:1], b_iota,
                                           ALU.add))
                        C(eg.tensor_tensor(b_tmp[:, 0:1], b_tmp[:, 0:1], b_big,
                                           ALU.subtract))
                        C(eg.tensor_tensor(b_tmp[:, 0:1], b_tmp[:, 0:1],
                                           b_stat[:, 5:6], ALU.mult))
                        C(eg.tensor_tensor(b_tmp[:, 1:2], b_tmp[:, 0:1], b_big,
                                           ALU.add))
                        C(eg.tensor_copy(b_eprev, b_ecur))
                        return eg.tensor_copy(b_widx[pr % 2], b_tmp[:, 1:2])
                    chain_v[("w", pr)] = step(
                        "vector", [chain_v[("s1", tau)],
                                   w_g.get(("w2b", pr - 2))], f)
                    wwait = [chain_v[("w", pr)],
                             mm2_m.get((2 * pr - 1, 1))]
                    w_g[("w1a", pr)] = idma(
                        "wg", b_w1buf[:, 0:16384], None, s_w1s,
                        bass.IndirectOffsetOnAxis(ap=b_widx[pr % 2], axis=0),
                        waits=wwait, bounds_check=E * 128 - 1, oob_is_err=False)
                    w_g[("w1b", pr)] = idma(
                        "wg", b_w1buf[:, 16384:32768], None, s_w1s,
                        bass.IndirectOffsetOnAxis(ap=b_widx[pr % 2], axis=0),
                        element_offset=16384,
                        bounds_check=E * 128 - 1, oob_is_err=False)
                    b1_g[pr] = idma(
                        "wg", b_b1c[0], None, s_b1T,
                        bass.IndirectOffsetOnAxis(ap=b_widx[pr % 2], axis=0),
                        bounds_check=E * 128 - 1, oob_is_err=False)
                    w_g[("w2a", pr)] = idma(
                        "wg", b_w2buf[:, 0:16384], None, s_w2s,
                        bass.IndirectOffsetOnAxis(ap=b_widx[pr % 2], axis=0),
                        bounds_check=E * 128 - 1, oob_is_err=False)
                    w_g[("w2b", pr)] = idma(
                        "wg", b_w2buf[:, 16384:32768], None, s_w2s,
                        bass.IndirectOffsetOnAxis(ap=b_widx[pr % 2], axis=0),
                        element_offset=16384,
                        bounds_check=E * 128 - 1, oob_is_err=False)

                chain1(0)
                chain1(1)
                chain1(2)
                chain1(3)
                wfetch(0)
                for pr in range(NPAIR):
                    for tt in range(2):
                        tau = 2 * pr + tt
                        for jc in range(NC):
                            sl = len(tplist) % 2
                            w = [xnp_g[tau],
                                 tplist[-2] if len(tplist) >= 2 else None]
                            mv = step("tensor", w, lambda eg, jc=jc, tau=tau, sl=sl:
                                      eg.transpose(
                                          p_tp2[sl],
                                          b_xnP[tau % 4][:, jc * 128:(jc + 1) * 128],
                                          b_identb))
                            tp_m[(tau, jc)] = mv
                            tp_v[(tau, jc)] = step(
                                "vector", [mv,
                                           mm1_m.get((tau // 2 - 2, NF - 1))],
                                lambda eg, jc=jc, tt=tt, tau=tau, sl=sl:
                                eg.tensor_copy(
                                    b_xnPT[(tau // 2) % 2][:, jc * 256 + tt * 128:
                                                           jc * 256 + (tt + 1) * 128],
                                    p_tp2[sl]))
                            tplist.append(tp_v[(tau, jc)])
                    xptr = [tp_v[(2 * pr + tt, jc)]
                            for tt in range(2) for jc in range(NC)]
                    for Fc in range(NF):
                        w = [w_g[("w2b", pr)], rl_m.get((pr, Fc - 2))]
                        if Fc == 0:
                            w += xptr

                        def mm1(eg, pr=pr, Fc=Fc):
                            r = None
                            for jc in range(NC):
                                r = eg.matmul(
                                    p_h[Fc % 2],
                                    b_w1buf[:, jc * FF + Fc * 128:
                                            jc * FF + (Fc + 1) * 128],
                                    b_xnPT[pr % 2][:, jc * 256:(jc + 1) * 256],
                                    start=(jc == 0), stop=(jc == NC - 1))
                            return r
                        mm1_m[(pr, Fc)] = step("tensor", w, mm1)
                        rl_m[(pr, Fc)] = step(
                            "scalar", [mm1_m[(pr, Fc)], w_g[("w2b", pr)]],
                            lambda eg, pr=pr, Fc=Fc: eg.activation(
                                b_h[:, Fc * 256:(Fc + 1) * 256], p_h[Fc % 2],
                                AF.Relu, bias=b_b1c[0][:, Fc:Fc + 1]))
                    for tt in range(2):
                        tau = 2 * pr + tt
                        for hf in range(2):
                            k = (tt * 2 + hf) % 2
                            w = [rl_m[(pr, NF - 1)], w_g[("w2b", pr)],
                                 yev.get((tau - 1, hf))]

                            def mm2(eg, tt=tt, hf=hf, k=k):
                                r = None
                                for Fc in range(NF):
                                    r = eg.matmul(
                                        p_o[k],
                                        b_h[:, Fc * 256 + tt * 128:
                                            Fc * 256 + (tt + 1) * 128],
                                        b_w2buf[:, Fc * D + hf * 512:
                                                Fc * D + (hf + 1) * 512],
                                        start=(Fc == 0), stop=(Fc == NF - 1))
                                return r
                            mm2_m[(tau, hf)] = step("tensor", w, mm2)
                            yev[(tau, hf)] = step(
                                "vector", [mm2_m[(tau, hf)],
                                           sct.get(tau - 2),
                                           chain_v[("g", tau)]],
                                lambda eg, tau=tau, hf=hf, k=k: eg.tensor_scalar(
                                    b_y[tau % 2][:, hf * 512:(hf + 1) * 512],
                                    p_o[k], b_gp[tau % 4], None, ALU.mult))
                        sct[tau] = idma(
                            "st%d" % (tau % 2), s_pairs, bass.IndirectOffsetOnAxis(
                                ap=b_rec[tau % 4][:, 1:2], axis=0),
                            b_y[tau % 2], None,
                            waits=[yev[(tau, 1)]],
                            bounds_check=2 * T - 1, oob_is_err=False)
                    if pr + 1 < NPAIR:
                        wfetch(pr + 1)
                    if pr + 2 < NPAIR:
                        chain1(2 * pr + 4)
                        chain1(2 * pr + 5)
                for e in range(E):
                    wv2 = dma("sync", "ce", b_w2buf, s_w2s[e * 128:(e + 1) * 128, :],
                              waits=[(MM, MM.val)])
                    bv1 = dma("sync", "ce", b_b1c[0], s_b1T[e * 128:(e + 1) * 128, :],
                              waits=[(ACS, ACS.val)])
                    bv2 = dma("sync", "ce", b_b2r,
                              db2[e].rearrange("(o d) -> o d", o=1),
                              waits=[(VE, VE.val)])
                    r1 = step("scalar", [bv2], lambda eg: eg.activation(
                        b_rb1, b_b1c[0], AF.Relu))
                    for hf in range(2):
                        def cemm(eg, hf=hf):
                            r = None
                            for Fc in range(NF):
                                r = eg.matmul(
                                    p_ce, b_rb1[:, Fc:Fc + 1],
                                    b_w2buf[:, Fc * D + hf * 512:
                                            Fc * D + (hf + 1) * 512],
                                    start=(Fc == 0), stop=(Fc == NF - 1))
                            return r
                        mv = step("tensor", [r1, (VE, VE.val)], cemm)
                        step("vector", [mv],
                             lambda eg, hf=hf: (
                                 C(eg.tensor_copy(b_cestg, p_ce)),
                                 C(eg.scalar_tensor_tensor(
                                     b_ccstg[0:1, hf * 512:(hf + 1) * 512],
                                     b_b2r[0:1, hf * 512:(hf + 1) * 512], 1.0,
                                     p_ce, ALU.mult, ALU.subtract)),
                                 eg.tensor_tensor(
                                     b_ctacc[0:1, hf * 512:(hf + 1) * 512],
                                     b_ctacc[0:1, hf * 512:(hf + 1) * 512],
                                     b_cestg, ALU.add))[-1])
                    dma("scalar", "ce", s_cc[e].rearrange("(o d) -> o d", o=1),
                        b_ccstg, waits=[(VE, VE.val)])
                dma("scalar", "ce", s_cc[E].rearrange("(o d) -> o d", o=1), b_ctacc,
                    waits=[(VE, VE.val)])
            return body

        if stop_after == 'R':
            return nc, dbg
        run_phase(phaseF)


        def phaseC(ph):
            b_cb = [sbuf(f"pc_cb{i}", [128, D], F32, ph) for i in range(2)]
            b_p1 = [sbuf(f"pc_p1{i}", [128, D], F32, ph) for i in range(2)]
            b_l2 = [sbuf(f"pc_l2{i}", [128, D], F32, ph) for i in range(2)]
            b_x2 = [sbuf(f"pc_x2{i}", [128, D], F32, ph) for i in range(2)]
            b_f1 = [sbuf(f"pc_f1{i}", [128, D], F32, ph) for i in range(2)]
            b_f2 = sbuf("pc_f2", [128, D], F32, ph)
            b_fo = [sbuf(f"pc_fo{i}", [128, D], F32, ph) for i in range(2)]
            b_g9 = sbuf("pc_g9", [128, 128], F32, ph)
            b_g9T = sbuf("pc_g9T", [128, 128], F32, ph)
            b_junk = sbuf("pc_junk", [128, D], F32, ph)
            p_g9 = ph.enter_context(nc.psum_tensor("pc_pg9", [128, 128], F32))[:]
            p_c = [ph.enter_context(nc.psum_tensor(f"pc_c{i}", [128, 512], F32))[:]
                   for i in range(2)]

            def body(he):
                own, dma, idma, step = he.own, he.dma, he.idma, he.step
                C = he.C
                ccv = dma("sync", "ccl", b_cc2, s_cc)
                if t_cc is not None:
                    dma("sync", "taps", t_cc, s_cc)
                if t_pairs is not None:
                    for t in range(2 * NT):
                        dma("sync", "taps", t_pairs[t * 128:(t + 1) * 128, :],
                            s_pairs[t * 128:(t + 1) * 128, :])
                ld_m, st_m, fdone = {}, {}, {}
                g9done = {}
                corr_ev = []
                for t in range(NT):
                    pl = "pl%d" % (t % 2)
                    l1 = dma("sync", pl, b_cb[t % 2],
                             s_pairs[t * 128:(t + 1) * 128, :],
                             waits=[fdone.get(t - 2)])
                    l2 = dma("sync", pl, b_p1[t % 2],
                             s_pairs[(T + t * 128):(T + (t + 1) * 128), :])
                    l3 = dma("sync", pl, b_l2[t % 2], s_ln2[t])
                    l4 = dma("sync", pl, b_x2[t % 2], s_x2[t])
                    gv = step("vector", [g9done.get(t - 1)],
                              lambda eg, t=t: (
                                  C(eg.tensor_copy(b_g9[:, 0:8],
                                                   b_g[:, t * 8:(t + 1) * 8])),
                                  C(eg.memset(b_g9[:, 9:16], 0.0)),
                                  eg.tensor_copy(b_g9[:, 8:9], b_G[:, t:t + 1]))[-1])
                    tv = step("tensor", [gv], lambda eg: eg.transpose(
                        p_g9, b_g9, b_identf))
                    ge = step("vector", [tv], lambda eg: eg.tensor_copy(
                        b_g9T[0:16, :], p_g9[0:16, :]))
                    for hf in range(2):
                        w = [ge, ccv]
                        if len(corr_ev) >= 2:
                            w.append(corr_ev[-2])
                        mv = step("tensor", w, lambda eg, hf=hf: eg.matmul(
                            p_c[hf], b_g9T[0:9, :],
                            b_cc2[0:9, hf * 512:(hf + 1) * 512],
                            start=True, stop=True))
                        g9done[t] = mv
                        ev = step("vector", [mv, l4],
                                  lambda eg, t=t, hf=hf: (
                                      C(eg.tensor_tensor(
                                          b_f1[t % 2][:, hf * 512:(hf + 1) * 512],
                                          b_cb[t % 2][:, hf * 512:(hf + 1) * 512],
                                          b_p1[t % 2][:, hf * 512:(hf + 1) * 512],
                                          ALU.add)),
                                      C(eg.tensor_tensor(
                                          b_f1[t % 2][:, hf * 512:(hf + 1) * 512],
                                          b_f1[t % 2][:, hf * 512:(hf + 1) * 512],
                                          b_l2[t % 2][:, hf * 512:(hf + 1) * 512],
                                          ALU.add)),
                                      eg.scalar_tensor_tensor(
                                          b_f1[t % 2][:, hf * 512:(hf + 1) * 512],
                                          p_c[hf], 1.0,
                                          b_f1[t % 2][:, hf * 512:(hf + 1) * 512],
                                          ALU.mult, ALU.add,
                                          accum_out=b_stat[:, 8 + hf:9 + hf]))[-1])
                        corr_ev.append(ev)
                    sq = step("vector", [corr_ev[-1]], lambda eg, t=t: (
                        C(eg.tensor_tensor(b_stat[:, 0:1], b_stat[:, 8:9],
                                           b_stat[:, 9:10], ALU.add)),
                        eg.scalar_tensor_tensor(
                            b_junk, b_f1[t % 2], 1.0, b_f1[t % 2],
                            ALU.mult, ALU.mult, accum_out=b_stat[:, 1:2]))[-1])
                    y2 = ln_seq(he, b_f1[t % 2], b_f2, None, waits=[sq],
                                sums_precomputed=True)
                    fdone[t] = step("vector", [y2, st_m.get(t - 2)],
                                    lambda eg, t=t: eg.tensor_tensor(
                                        b_fo[t % 2], b_f2, b_x2[t % 2], ALU.add))
                    st_m[t] = dma("scalar", "ou%d" % (t % 2),
                                  dout[t * 128:(t + 1) * 128, :],
                                  b_fo[t % 2], waits=[fdone[t]])
            return body

        if stop_after == 'F':
            return nc, dbg
        run_phase(phaseC)

    return nc, dbg


def _shard_inputs(inputs):
    x = np.ascontiguousarray(np.asarray(inputs["L0_attn_x"], dtype=np.float32))
    names = dict(
        Wq="L0_attn_Wq", bq="L0_attn_bq", Wk="L0_attn_Wk", bk="L0_attn_bk",
        Wv="L0_attn_Wv", bv="L0_attn_bv", Wo="L0_attn_Wo", bo="L0_attn_bo",
        gamma="L0_ffn_gamma", beta="L0_ffn_beta", Wr="L0_ffn_Wr", br="L0_ffn_br",
        W1="L0_ffn_W1", b1="L0_ffn_b1", W2="L0_ffn_W2", b2="L0_ffn_b2")
    shared = {k: np.ascontiguousarray(np.asarray(inputs[v], dtype=np.float32))
              for k, v in names.items()}
    return [dict(x=x[b], **shared) for b in range(8)]


def kernel(**inputs):
    in_maps = _shard_inputs(inputs)
    nc, _ = build_program()
    res = run_bass_kernel_spmd(nc, in_maps, core_ids=list(range(8)))
    return np.stack([r["out"] for r in res.results], axis=0)
